# version 01
# baseline (speedup 1.0000x reference)
import contextlib
import numpy as np
import ml_dtypes
import concourse.bass as bass
import concourse.mybir as mybir
from concourse.bass_utils import run_bass_kernel_spmd

F32 = mybir.dt.float32
BF16 = mybir.dt.bfloat16
I32 = mybir.dt.int32
AF = mybir.ActivationFunctionType
ALU = mybir.AluOpType
AXL = mybir.AxisListType.X
BF = ml_dtypes.bfloat16

D = 2048
S = 4096
HS = 2048
NH = 16
DH = 128
BAND = 409
ISD = float(1.0 / np.sqrt(DH))
BIGF = 1.0e30
EPS_LN = 1e-5
NBIS = 40

_UPTO = "full"
_DEBUG = False


def _perm_heads():
    pi = np.empty(DH, np.int64)
    pi[0::2] = np.arange(64)
    pi[1::2] = np.arange(64) + 64
    return np.concatenate([h * DH + pi for h in range(NH)])


def _host_consts():
    c = {}
    c["idf"] = np.eye(128, dtype=np.float32)
    c["cumtri"] = np.triu(np.ones((128, 128), np.float32)).copy()
    c["onescolb"] = np.ones((128, 1), dtype=BF)
    c["onescolf"] = np.ones((128, 1), dtype=np.float32)
    c["onesrowf"] = np.ones((1, 128), dtype=np.float32)
    e8 = np.zeros((128, 16), np.float32)
    for p in range(128):
        e8[p, p // 8] = 1.0
    c["e8"] = e8
    c["e8t"] = np.ascontiguousarray(e8.T)
    rep = np.zeros((16, 256), np.float32)
    for er in range(256):
        rep[er // 16, er] = 1.0
    c["rep16"] = rep
    dk = np.zeros((128, 128), np.float32)
    for p in range(128):
        dk[p, p + 1:] = -BIGF
    c["dkill"] = dk
    jj = np.arange(512)[None, :]
    pp = np.arange(128)[:, None]
    for dlt in range(-384, 513, 128):
        r = jj + dlt - pp
        c[f"bc_{dlt}"] = ((r >= 0) & (r <= BAND)).astype(BF)
    c["cz_0"] = ((jj - pp) >= 0).astype(BF)
    return c


def _rope_tables(tbase):
    invf = (1.0 / 10000.0 ** (np.arange(0, DH, 2, dtype=np.float32) / DH)).astype(np.float32)
    pos = np.arange(tbase, tbase + HS, dtype=np.float32)
    fr = pos[None, :] * invf[:, None]
    cos = np.cos(fr).astype(np.float32)
    sin = np.sin(fr).astype(np.float32)
    rc = np.empty((128, HS), np.float32)
    rs = np.empty((128, HS), np.float32)
    rc[0::2] = cos
    rc[1::2] = cos
    rs[0::2] = -sin
    rs[1::2] = sin
    return rc, rs


SHUF_SWAP = []
for _g in range(16):
    SHUF_SWAP += [2 * _g + 1, 2 * _g]


_HZ_OPS = frozenset("""tensor_tensor tensor_add tensor_sub tensor_mul tensor_max
tensor_scalar tensor_single_scalar tensor_scalar_add tensor_scalar_sub
tensor_scalar_mul tensor_scalar_max tensor_scalar_min tensor_relu tensor_reduce
reduce_max reduce_sum scalar_tensor_tensor tensor_tensor_scan tensor_copy memset
memzero reciprocal select transpose stream_shuffle bn_stats bn_aggr max max_index
pool pool_avg pool_max match_replace copy_predicated activation mul add copy sign
square sqrt iota partition_broadcast affine_select""".split())


class HZ:

    def __init__(self, eng):
        self._e = eng

    def __getattr__(self, n):
        a = getattr(self._e, n)
        if n in _HZ_OPS and callable(a):
            e = self._e

            def w(*args, **kwargs):
                r = a(*args, **kwargs)
                e.drain()
                return r
            return w
        return a


class Sem:

    def __init__(self, stack, nc, name):
        self.h = stack.enter_context(nc.semaphore(name))
        self.v = 0

    def inc(self, instr, n=1):
        instr.then_inc(self.h, n)
        self.v += n
        return instr

    def wait(self, eng):
        eng.wait_ge(self.h, self.v)

    def rinc(self, q, key, fn, args, n=1, kwargs=None):
        h = self.h
        self.v += n
        q[key].append((fn, args, kwargs or {}, h, n))

    def rwait(self, q, key, eng):
        h, v = self.h, self.v
        q[key].append((eng.wait_ge, (h, v), {}, None, 0))


def _replay(q, key):
    for fn, args, kwargs, semh, n in q[key]:
        r = fn(*args, **kwargs)
        if semh is not None:
            r.then_inc(semh, n)


def build_nc(upto="full", debug=False):
    nc = bass.Bass("TRN2", target_bir_lowering=False)
    dram = lambda n, s, d, k="ExternalInput": nc.dram_tensor(n, s, d, kind=k)

    ins = {}
    for n, s, d in [
        ("xo", [D, HS], F32), ("xh", [D, HS], F32), ("role", [1, 1], I32),
        ("wq", [D, D], F32), ("wk", [D, D], F32), ("wv", [D, D], F32),
        ("wo", [D, D], F32), ("w0", [D, D], F32), ("w1", [D, D], F32),
        ("rw1", [D, 64], F32), ("rw2", [64, 16], F32),
        ("at", [D, 256], F32), ("bt", [256, D], F32),
        ("cT", [64, D], F32), ("bT", [D, 64], F32),
        ("dcol", [128, 16], F32), ("b0col", [128, 16], F32), ("b1col", [128, 16], F32),
        ("lamur", [1, 32], F32), ("lamim", [1, 32], F32), ("logd", [1, 32], F32),
        ("rb1r", [1, 64], F32), ("rb2r", [1, 16], F32), ("gater", [1, 16], F32),
        ("ropec", [128, HS], F32), ("ropes", [128, HS], F32),
        ("idf", [128, 128], F32), ("cumtri", [128, 128], F32),
        ("onescolb", [128, 1], BF16), ("onescolf", [128, 1], F32),
        ("onesrowf", [1, 128], F32),
        ("e8", [128, 16], F32), ("e8t", [16, 128], F32), ("rep16", [16, 256], F32),
        ("dkill", [128, 128], F32), ("cz_0", [128, 512], BF16),
    ]:
        ins[n] = dram(n, s, d)
    for dlt in range(-384, 513, 128):
        ins[f"bc_{dlt}"] = dram(f"bc_{dlt}", [128, 512], BF16)

    out = dram("out", [D, HS], F32, "ExternalOutput")
    kd = "ExternalOutput" if debug else "Internal"
    x1s = nc.dram_tensor("x1s", [D, HS], F32, kind=kd)
    xn1s = nc.dram_tensor("xn1s", [D, HS], BF16, kind=kd)
    qscr = nc.dram_tensor("qscr", [D, HS], BF16, kind=kd)
    bscrd = nc.dram_tensor("bscrd", [D, HS], F32, kind="Internal")
    cc_kv_i = nc.dram_tensor("cc_kv_i", [4224, D], BF16)
    cc_kv_o = nc.dram_tensor("cc_kv_o", [8448, D], BF16)
    cc_t_i = nc.dram_tensor("cc_t_i", [16, S], F32)
    cc_t_o = nc.dram_tensor("cc_t_o", [32, S], F32)
    ckvo_d = nc.dram_tensor("ckvo_d", [8448, D], BF16, kind=kd) if debug else None
    cto_d = nc.dram_tensor("cto_d", [32, S], F32, kind=kd) if debug else None
    attnd = nc.dram_tensor("attnd", [D, HS], BF16, kind=kd)
    xnd = nc.dram_tensor("xnd", [D, HS], BF16, kind=kd)
    f0d = nc.dram_tensor("f0d", [D, HS], F32, kind=kd)
    y0d = nc.dram_tensor("y0d", [D, HS], F32, kind=kd)
    dbgrow = nc.dram_tensor("dbgrow", [1, 1024], F32, kind=kd)
    dbgtile = nc.dram_tensor("dbgtile", [128, 1024], F32, kind=kd)

    est = contextlib.ExitStack()
    with est:
        sbg = lambda n, s, d: est.enter_context(nc.sbuf_tensor(n, s, d))
        pst = lambda n, s: est.enter_context(nc.psum_tensor(n, s, F32))
        mks = lambda n: Sem(est, nc, n)

        C = {}
        for n, s, d in [
            ("idf", [128, 128], F32), ("cumtri", [128, 128], F32),
            ("onescolb", [128, 1], BF16), ("onescolf", [128, 1], F32),
            ("onesrowf", [1, 128], F32),
            ("e8", [128, 16], F32), ("e8t", [16, 128], F32), ("rep16", [16, 256], F32),
            ("dkill", [128, 128], F32), ("dcol", [128, 16], F32),
            ("b0col", [128, 16], F32), ("b1col", [128, 16], F32),
            ("role", [1, 1], I32), ("cz_0", [128, 512], BF16),
        ]:
            C[n] = sbg("c_" + n, s, d)
        for dlt in range(-384, 513, 128):
            C[f"bc_{dlt}"] = sbg(f"c_bc{dlt}", [128, 512], BF16)

        srow = sbg("srow", [1, 1024], F32)
        sml = sbg("sml", [1, 1024], F32)
        p_neg = sbg("p_neg", [128, 64], F32)
        p_pos = sbg("p_pos", [128, 64], F32)
        p_pos1 = sbg("p_pos1", [128, 64], F32)
        a128 = sbg("a128", [1, 64], F32)
        a127 = sbg("a127", [1, 64], F32)
        wbT = sbg("wbT", [128, 16 * 64], BF16)
        cTb = sbg("cTb", [64, D], BF16)
        epsr = sbg("epsr", [1, 1], F32)
        onesall = sbg("onesall", [128, 128], F32)
        WB = sbg("WB", [128, 16 * 2048], BF16)
        WS = [sbg(f"WS{i}", [128, 2048], BF16) for i in range(2)]
        WS2 = [sbg(f"WS2_{i}", [128, 2048], BF16) for i in range(2)]
        EV = [sbg(f"EV{i}", [128, 512], F32) for i in range(3)]
        EVB = sbg("EVB", [128, 2048], BF16)
        scl = sbg("scl", [128, 4], F32)
        rw_a = sbg("rw_a", [1, 512], F32)
        rw_b = sbg("rw_b", [1, 512], F32)
        rw_c = sbg("rw_c", [1, 512], F32)
        negc = sbg("negc", [128, 1], F32)
        negcB = sbg("negcB", [128, 1], F32)
        zacc = sbg("zacc", [128, 1], F32)
        zprt = sbg("zprt", [128, 1], F32)
        invz = sbg("invz", [128, 1], F32)

        pA = [pst(f"pA{i}", [128, 512]) for i in range(2)]
        pB = [pst(f"pB{i}", [128, 512]) for i in range(2)]
        pC = pst("pC", [128, 512])
        pD = pst("pD", [128, 512])
        pE = pst("pE", [128, 512])
        pF = pst("pF", [128, 512])

        sy = mks("sy")
        sg = mks("sg")
        ss = mks("ss")
        sv = mks("sv")
        stn = mks("stn")
        ssc = mks("ssc")
        scc = mks("scc")
        bar = mks("bar")

        V, T, SC, SY, G = HZ(nc.vector), nc.tensor, HZ(nc.scalar), nc.sync, HZ(nc.gpsimd)
        ALL_E = (V, T, SC, SY, G)

        def barrier():
            for e in ALL_E:
                bar.inc(e.nop())
            for e in ALL_E:
                bar.wait(e)

        for n in ("idf", "cumtri", "onescolb", "onescolf", "onesrowf", "e8", "e8t",
                  "rep16", "dkill", "dcol", "b0col", "b1col", "role", "cz_0"):
            sy.inc(SY.dma_start(out=C[n][:], in_=ins[n].ap()), 16)
        for dlt in range(-384, 513, 128):
            sy.inc(SY.dma_start(out=C[f"bc_{dlt}"][:], in_=ins[f"bc_{dlt}"].ap()), 16)
        for dst, name in [((0, 32), "lamur"), ((32, 64), "lamim"), ((64, 96), "logd"),
                          ((96, 160), "rb1r"), ((160, 176), "rb2r"), ((176, 192), "gater")]:
            sy.inc(SY.dma_start(out=srow[0:1, dst[0]:dst[1]], in_=ins[name].ap()), 16)

        rreg = {}
        for e in ALL_E:
            sy.wait(e)
            r = e.alloc_register("rl")
            e.reg_load(r, C["role"][0:1, 0:1])
            rreg[id(e)] = r
        V.memset(epsr[:], EPS_LN)
        V.memset(onesall[:], 1.0)

        with contextlib.ExitStack() as es0:
            sb0 = lambda n, s, d: es0.enter_context(nc.sbuf_tensor(n, s, d))
            pw_re = sb0("pw_re", [32, 160], F32)
            pw_im = sb0("pw_im", [32, 160], F32)
            gcol = sb0("gcol", [32, 8], F32)
            gb = sb0("gb", [128, 128], F32)
            wbf = sb0("wbf", [128, 64], F32)
            bTf = sb0("bTf", [128, 16 * 64], F32)
            cTf = sb0("cTf", [64, D], F32)
            WF = [sb0(f"WF{i}", [128, 2048], F32) for i in range(2)]

            for ks in range(16):
                sy.inc(SY.dma_start(out=bTf[:, 64 * ks:64 * ks + 64],
                                    in_=ins["bT"].ap()[128 * ks:128 * ks + 128, :]), 16)
            sy.inc(SY.dma_start(out=cTf[:], in_=ins["cT"].ap()), 16)

            rr = lambda a, w=32: srow[0:1, a:a + w]
            tt = lambda a, w=32: sml[0:1, a:a + w]
            LUR, LIM, LGD = 0, 32, 64
            DEL, LRE, TH, ARE, AIM, GRE, GIM, L2 = 192, 224, 256, 288, 320, 352, 384, 416

            sy.wait(SC)
            sy.wait(V)
            ssc.inc(SC.activation(rr(DEL), rr(LGD), AF.Exp))
            ssc.inc(SC.activation(tt(0), rr(LUR), AF.Exp))
            ssc.wait(V)
            sv.inc(V.tensor_scalar(tt(0), tt(0), 1.0, None, ALU.add))
            sv.wait(SC)
            ssc.inc(SC.activation(tt(32), tt(0), AF.Ln))
            ssc.wait(V)
            sv.inc(V.tensor_scalar(rr(LRE), tt(32), 0.0101, -1.0, ALU.add, ALU.mult))
            sv.inc(V.tensor_tensor(rr(TH), rr(LIM), rr(DEL), ALU.mult))
            sv.inc(V.tensor_copy(srow[0:1, 672:704], rr(TH)))
            i2p, tp = float(1.0 / (2 * np.pi)), float(2 * np.pi)
            for _it in range(2):
                sv.inc(V.tensor_scalar(tt(0), rr(TH), i2p, 0.5, ALU.mult, ALU.is_ge))
                sv.inc(V.tensor_scalar(tt(32), rr(TH), -i2p, 0.5, ALU.mult, ALU.is_ge))
                if _it == 0:
                    sv.inc(V.tensor_copy(srow[0:1, 704:736], tt(0)))
                    sv.inc(V.tensor_copy(srow[0:1, 736:768], tt(32)))
                sv.inc(V.tensor_tensor(tt(0), tt(0), tt(32), ALU.subtract))
                sv.inc(V.tensor_scalar(tt(0), tt(0), tp, None, ALU.mult))
                sv.inc(V.tensor_tensor(rr(TH), rr(TH), tt(0), ALU.subtract))
            sv.inc(V.tensor_copy(srow[0:1, 448:480], rr(TH)))
            sv.wait(SC)
            ssc.inc(SC.activation(tt(64), rr(TH), AF.Sin, scale=0.5))
            ssc.wait(V)
            sv.inc(V.tensor_copy(srow[0:1, 480:512], tt(64)))
            sv.inc(V.tensor_scalar(tt(0), rr(TH), 0.5, None, ALU.mult))
            sv.inc(V.tensor_scalar(tt(32), rr(TH), -0.5, None, ALU.mult))
            sv.inc(V.tensor_tensor(tt(0), tt(0), tt(32), ALU.max))
            sv.inc(V.tensor_scalar(tt(0), tt(0), -1.0, float(np.pi / 2), ALU.mult, ALU.add))
            sv.inc(V.tensor_copy(srow[0:1, 640:672], tt(0)))
            sv.wait(SC)
            ssc.inc(SC.activation(tt(96), tt(0), AF.Sin))
            ssc.wait(V)
            sv.inc(V.tensor_copy(srow[0:1, 512:544], tt(96)))
            sv.inc(V.tensor_tensor(tt(0), rr(LRE), rr(DEL), ALU.mult))
            sv.wait(SC)
            ssc.inc(SC.activation(tt(128), tt(0), AF.Exp))
            ssc.wait(V)
            sv.inc(V.tensor_tensor(tt(0), tt(64), tt(64), ALU.mult))
            sv.inc(V.tensor_scalar(tt(0), tt(0), -2.0, 1.0, ALU.mult, ALU.add))
            sv.inc(V.tensor_tensor(tt(32), tt(64), tt(96), ALU.mult))
            sv.inc(V.tensor_scalar(tt(32), tt(32), 2.0, None, ALU.mult))
            sv.inc(V.tensor_copy(srow[0:1, 576:608], tt(0)))
            sv.inc(V.tensor_copy(srow[0:1, 608:640], tt(32)))
            sv.inc(V.tensor_tensor(rr(ARE), tt(128), tt(0), ALU.mult))
            sv.inc(V.tensor_tensor(rr(AIM), tt(128), tt(32), ALU.mult))
            sv.inc(V.tensor_tensor(tt(0), rr(LRE), rr(LRE), ALU.mult))
            sv.inc(V.tensor_tensor(tt(32), rr(LIM), rr(LIM), ALU.mult))
            sv.inc(V.tensor_tensor(rr(L2), tt(0), tt(32), ALU.add))
            sv.inc(V.reciprocal(rr(L2), rr(L2)))
            sv.inc(V.tensor_scalar(tt(0), rr(ARE), -1.0, None, ALU.add))
            sv.inc(V.tensor_tensor(tt(32), tt(0), rr(LRE), ALU.mult))
            sv.inc(V.tensor_tensor(tt(64), rr(AIM), rr(LIM), ALU.mult))
            sv.inc(V.tensor_tensor(rr(GRE), tt(32), tt(64), ALU.add))
            sv.inc(V.tensor_tensor(rr(GRE), rr(GRE), rr(L2), ALU.mult))
            sv.inc(V.tensor_tensor(tt(32), rr(AIM), rr(LRE), ALU.mult))
            sv.inc(V.tensor_tensor(tt(64), tt(0), rr(LIM), ALU.mult))
            sv.inc(V.tensor_tensor(rr(GIM), tt(32), tt(64), ALU.subtract))
            sv.inc(V.tensor_tensor(rr(GIM), rr(GIM), rr(L2), ALU.mult))

            def row2col(row, col):
                sv.wait(T)
                stn.inc(T.transpose(pF[0:32, 0:1], row, C["idf"][0:1, 0:1]))
                stn.wait(V)
                sv.inc(V.tensor_copy(col, pF[0:32, 0:1]))

            row2col(rr(ARE), gcol[:, 0:1])
            row2col(rr(AIM), gcol[:, 1:2])

            sv.inc(V.memset(pw_re[:, 0:1], 1.0))
            sv.inc(V.memset(pw_im[:, 0:1], 0.0))
            sv.inc(V.tensor_copy(pw_re[:32, 1:2], gcol[:, 0:1]))
            sv.inc(V.tensor_copy(pw_im[:32, 1:2], gcol[:, 1:2]))
            m = 2
            while m < 137:
                w = min(m, 137 - m)
                sv.inc(V.tensor_scalar(gcol[:, 2:3], pw_re[:32, m - 1:m], gcol[:, 0:1], None, ALU.mult))
                sv.inc(V.tensor_scalar(gcol[:, 3:4], pw_im[:32, m - 1:m], gcol[:, 1:2], None, ALU.mult))
                sv.inc(V.tensor_tensor(gcol[:, 2:3], gcol[:, 2:3], gcol[:, 3:4], ALU.subtract))
                sv.inc(V.tensor_scalar(gcol[:, 4:5], pw_re[:32, m - 1:m], gcol[:, 1:2], None, ALU.mult))
                sv.inc(V.tensor_scalar(gcol[:, 5:6], pw_im[:32, m - 1:m], gcol[:, 0:1], None, ALU.mult))
                sv.inc(V.tensor_tensor(gcol[:, 4:5], gcol[:, 4:5], gcol[:, 5:6], ALU.add))
                sv.inc(V.tensor_scalar(pw_re[:32, m:m + w], pw_re[:32, 0:w], gcol[:, 2:3], None, ALU.mult))
                sv.inc(V.tensor_scalar(wbf[0:32, 0:w], pw_im[:32, 0:w], gcol[:, 4:5], None, ALU.mult))
                sv.inc(V.tensor_tensor(pw_re[:32, m:m + w], pw_re[:32, m:m + w], wbf[0:32, 0:w], ALU.subtract))
                sv.inc(V.tensor_scalar(pw_im[:32, m:m + w], pw_re[:32, 0:w], gcol[:, 4:5], None, ALU.mult))
                sv.inc(V.tensor_scalar(wbf[0:32, 0:w], pw_im[:32, 0:w], gcol[:, 2:3], None, ALU.mult))
                sv.inc(V.tensor_tensor(pw_im[:32, m:m + w], pw_im[:32, m:m + w], wbf[0:32, 0:w], ALU.add))
                m += w
            ngt = sb0("ngt", [32, 384], F32)
            sv.inc(V.tensor_tensor(ngt[:, 256:384], pw_re[:32, 0:128], pw_re[:32, 0:128], ALU.mult))
            sv.inc(V.tensor_tensor(ngt[:, 0:128], pw_im[:32, 0:128], pw_im[:32, 0:128], ALU.mult))
            sv.inc(V.tensor_tensor(ngt[:, 256:384], ngt[:, 256:384], ngt[:, 0:128], ALU.add))
            sv.inc(V.reciprocal(ngt[:, 256:384], ngt[:, 256:384]))
            sv.inc(V.tensor_tensor(ngt[:, 0:128], pw_re[:32, 0:128], ngt[:, 256:384], ALU.mult))
            sv.inc(V.tensor_tensor(ngt[:, 128:256], pw_im[:32, 0:128], ngt[:, 256:384], ALU.mult))
            sv.inc(V.tensor_scalar(ngt[:, 128:256], ngt[:, 128:256], -1.0, None, ALU.mult))

            def tr32(src, dst):
                sv.wait(T)
                stn.inc(T.transpose(pF[0:128, 0:32], src, C["idf"][0:32, 0:32]))
                stn.wait(V)
                sv.inc(V.tensor_copy(dst, pF[0:128, 0:32]))

            tr32(ngt[:, 0:128], p_neg[:, 0:32])
            tr32(ngt[:, 128:256], p_neg[:, 32:64])
            tr32(pw_re[:32, 0:128], p_pos[:, 0:32])
            tr32(pw_im[:32, 0:128], p_pos[:, 32:64])
            tr32(pw_re[:32, 1:129], p_pos1[:, 0:32])
            tr32(pw_im[:32, 1:129], p_pos1[:, 32:64])
            sv.wait(T)
            stn.inc(T.transpose(pF[0:1, 0:32], pw_re[:32, 128:129], C["idf"][0:32, 0:32]))
            stn.inc(T.transpose(pF[0:1, 32:64], pw_im[:32, 128:129], C["idf"][0:32, 0:32]))
            stn.inc(T.transpose(pF[0:1, 64:96], pw_re[:32, 127:128], C["idf"][0:32, 0:32]))
            stn.inc(T.transpose(pF[0:1, 96:128], pw_im[:32, 127:128], C["idf"][0:32, 0:32]))
            stn.wait(V)
            sv.inc(V.tensor_copy(a128[:], pF[0:1, 0:64]))
            sv.inc(V.tensor_copy(a127[:], pF[0:1, 64:128]))

            sv.wait(T)
            stn.inc(T.matmul(pF[0:128, 0:32], C["onesrowf"][:, :], rr(GRE), start=True, stop=True))
            stn.inc(T.matmul(pF[0:128, 32:64], C["onesrowf"][:, :], rr(GIM), start=True, stop=True))
            stn.wait(V)
            sy.wait(V)
            sv.inc(V.tensor_copy(gb[:, 0:64], pF[0:128, 0:64]))
            for ks in range(16):
                b_re = bTf[:, 64 * ks:64 * ks + 32]
                b_im = bTf[:, 64 * ks + 32:64 * ks + 64]
                sv.inc(V.tensor_tensor(wbf[:, 0:32], b_re, gb[:, 0:32], ALU.mult))
                sv.inc(V.tensor_tensor(wbf[:, 32:64], b_im, gb[:, 32:64], ALU.mult))
                sv.inc(V.tensor_tensor(wbf[:, 0:32], wbf[:, 0:32], wbf[:, 32:64], ALU.subtract))
                sv.inc(V.tensor_copy(wbT[:, 64 * ks:64 * ks + 32], wbf[:, 0:32]))
                sv.inc(V.tensor_tensor(wbf[:, 0:32], b_im, gb[:, 0:32], ALU.mult))
                sv.inc(V.tensor_tensor(wbf[:, 32:64], b_re, gb[:, 32:64], ALU.mult))
                sv.inc(V.tensor_tensor(wbf[:, 0:32], wbf[:, 0:32], wbf[:, 32:64], ALU.add))
                sv.inc(V.tensor_copy(wbT[:, 64 * ks + 32:64 * ks + 64], wbf[:, 0:32]))
            sv.inc(V.tensor_copy(cTb[:], cTf[:]))

            sv.inc(V.memset(zacc[:], 0.0))
            for ks in range(16):
                SY.wait_ge(sv.h, sv.v)
                sy.inc(SY.dma_start(out=WF[0][:], in_=ins["w0"].ap()[128 * ks:128 * ks + 128, :]), 16)
                sy.wait(V)
                sv.inc(V.tensor_reduce(zprt[:], WF[0][:], AXL, ALU.add, apply_absolute_value=True))
                sv.inc(V.tensor_tensor(zacc[:], zacc[:], zprt[:], ALU.add))
            sv.wait(T)
            stn.inc(T.matmul(pF[0:1, 0:1], C["onescolf"][:], zacc[:], start=True, stop=True))
            stn.wait(V)
            sv.inc(V.tensor_copy(srow[0:1, 768:769], pF[0:1, 0:1]))
            sv.inc(V.tensor_scalar(sml[0:1, 0:1], pF[0:1, 0:1], float(1.0 / (D * D)), None, ALU.mult))
            sv.inc(V.tensor_scalar(sml[0:1, 1:2], sml[0:1, 0:1], 1.0, 1e-10, ALU.mult, ALU.add))
            sv.inc(V.reciprocal(sml[0:1, 1:2], sml[0:1, 1:2]))
            sv.wait(T)
            stn.inc(T.matmul(pF[0:128, 0:2], C["onesrowf"][:, :], sml[0:1, 0:2], start=True, stop=True))
            stn.wait(V)
            sv.inc(V.tensor_copy(scl[:, 0:2], pF[0:128, 0:2]))
            sv.inc(V.tensor_scalar(scl[:, 2:3], scl[:, 1:2], -1.0, None, ALU.mult))
            for ks in range(16):
                b = ks % 2
                SY.wait_ge(sv.h, sv.v)
                sy.inc(SY.dma_start(out=WF[b][:], in_=ins["w0"].ap()[128 * ks:128 * ks + 128, :]), 16)
                sy.wait(V)
                sv.inc(V.tensor_scalar(WS[0][:], WF[b][:], scl[:, 1:2], 0.5, ALU.mult, ALU.is_ge))
                sv.inc(V.tensor_scalar(WS[1][:], WF[b][:], scl[:, 2:3], 0.5, ALU.mult, ALU.is_ge))
                sv.inc(V.tensor_tensor(WB[:, 2048 * ks:2048 * ks + 2048], WS[0][:], WS[1][:], ALU.subtract))
            SY.wait_ge(sy.h, sy.v)
            barrier()

        with contextlib.ExitStack() as es1:
            sb1 = lambda n, s, d: es1.enter_context(nc.sbuf_tensor(n, s, d))
            AX = sb1("AX", [128, 16 * 512], F32)
            AXB = sb1("AXB", [128, 16 * 512], BF16)
            AXN = sb1("AXN", [128, 16 * 512], BF16)
            buS = sb1("buS", [64, 512], F32)
            buT = sb1("buT", [128, 256], F32)
            xsT = sb1("xsT", [64, 512], BF16)
            carry = sb1("carry", [1, 64], F32)

            V.memset(carry[:], 0.0)

            def emit_ln(xt, xnt):
                sv.wait(T)
                sv.inc(V.tensor_copy(AXB[:], xt[:]))
                sv.wait(T)
                for ks in range(16):
                    stn.inc(T.matmul(pC[0:1, 0:512], C["onescolb"][:],
                                     AXB[:, 512 * ks:512 * ks + 512],
                                     start=(ks == 0), stop=(ks == 15)))
                sv.wait(SC)
                ssc.inc(SC.activation(AXB[:], AXB[:], AF.Square))
                ssc.wait(T)
                for ks in range(16):
                    stn.inc(T.matmul(pC[32:33, 0:512], C["onescolb"][:],
                                     AXB[:, 512 * ks:512 * ks + 512],
                                     start=(ks == 0), stop=(ks == 15), tile_position=(0, 32)))
                stn.wait(V)
                sv.inc(V.tensor_scalar(rw_a[:], pC[0:1, 0:512], float(1.0 / D), None, ALU.mult))
                sv.inc(V.tensor_scalar(rw_b[:], pC[32:33, 0:512], float(1.0 / D), None, ALU.mult))
                sv.inc(V.tensor_tensor(rw_c[:], rw_a[:], rw_a[:], ALU.mult))
                sv.inc(V.tensor_tensor(rw_b[:], rw_b[:], rw_c[:], ALU.subtract))
                sv.wait(SC)
                ssc.inc(SC.activation(rw_b[:], rw_b[:], AF.Ln, bias=epsr[0:1, 0:1]))
                ssc.inc(SC.activation(rw_b[:], rw_b[:], AF.Exp, scale=-0.5))
                ssc.wait(V)
                sv.inc(V.tensor_tensor(rw_a[:], rw_a[:], rw_b[:], ALU.mult))
                sv.wait(T)
                stn.inc(T.matmul(pD[0:128, 0:512], C["onesrowf"][:, :], rw_b[:], start=True, stop=True))
                stn.inc(T.matmul(pE[0:128, 0:512], C["onesrowf"][:, :], rw_a[:], start=True, stop=True))
                stn.wait(V)
                for ks in range(16):
                    sl = slice(512 * ks, 512 * ks + 512)
                    sv.inc(V.tensor_tensor(EV[0][:], xt[:, sl], pD[0:128, 0:512], ALU.mult))
                    sv.inc(V.tensor_tensor(xnt[:, sl], EV[0][:], pE[0:128, 0:512], ALU.subtract))

            def emit_scan(xnt, is_halo):
                sv.wait(T)
                for ks in range(16):
                    stn.inc(T.matmul(pF[0:64, 0:512], wbT[:, 64 * ks:64 * ks + 64],
                                     xnt[:, 512 * ks:512 * ks + 512],
                                     start=(ks == 0), stop=(ks == 15)))
                stn.wait(V)
                sv.inc(V.tensor_copy(buS[:], pF[0:64, 0:512]))
                sv.wait(T)
                for sub in range(4):
                    stn.inc(T.transpose(pF[0:128, 64 * sub:64 * sub + 64],
                                        buS[:, 128 * sub:128 * sub + 128], C["idf"][0:64, 0:64]))
                stn.wait(V)
                sv.inc(V.tensor_copy(buT[:], pF[0:128, 0:256]))
                for sub in range(4):
                    br = buT[:, 64 * sub:64 * sub + 32]
                    bi = buT[:, 64 * sub + 32:64 * sub + 64]
                    cre, cim = EV[0][:, 0:32], EV[0][:, 32:64]
                    t1, t2 = EV[0][:, 64:96], EV[0][:, 96:128]
                    sv.inc(V.tensor_tensor(cre, br, p_neg[:, 0:32], ALU.mult))
                    sv.inc(V.tensor_tensor(t1, bi, p_neg[:, 32:64], ALU.mult))
                    sv.inc(V.tensor_tensor(cre, cre, t1, ALU.subtract))
                    sv.inc(V.tensor_tensor(cim, br, p_neg[:, 32:64], ALU.mult))
                    sv.inc(V.tensor_tensor(t2, bi, p_neg[:, 0:32], ALU.mult))
                    sv.inc(V.tensor_tensor(cim, cim, t2, ALU.add))
                    sv.wait(T)
                    stn.inc(T.matmul(pF[0:128, 256:320], C["cumtri"][:], EV[0][:, 0:64],
                                     start=True, stop=True))
                    stn.inc(T.matmul(pE[0:1, 0:64], C["onescolf"][:], EV[0][:, 0:64],
                                     start=True, stop=True))
                    stn.wait(V)
                    lre, lim = EV[1][:, 0:32], EV[1][:, 32:64]
                    t3, t4 = EV[1][:, 64:96], EV[1][:, 96:128]
                    sv.inc(V.tensor_tensor(lre, pF[0:128, 256:288], p_pos[:, 0:32], ALU.mult))
                    sv.inc(V.tensor_tensor(t3, pF[0:128, 288:320], p_pos[:, 32:64], ALU.mult))
                    sv.inc(V.tensor_tensor(lre, lre, t3, ALU.subtract))
                    sv.inc(V.tensor_tensor(lim, pF[0:128, 256:288], p_pos[:, 32:64], ALU.mult))
                    sv.inc(V.tensor_tensor(t4, pF[0:128, 288:320], p_pos[:, 0:32], ALU.mult))
                    sv.inc(V.tensor_tensor(lim, lim, t4, ALU.add))
                    e5, e6 = sml[0:1, 224:256], sml[0:1, 256:288]
                    sv.inc(V.tensor_tensor(sml[0:1, 0:32], pE[0:1, 0:32], a127[0:1, 0:32], ALU.mult))
                    sv.inc(V.tensor_tensor(e5, pE[0:1, 32:64], a127[0:1, 32:64], ALU.mult))
                    sv.inc(V.tensor_tensor(sml[0:1, 0:32], sml[0:1, 0:32], e5, ALU.subtract))
                    sv.inc(V.tensor_tensor(sml[0:1, 32:64], pE[0:1, 0:32], a127[0:1, 32:64], ALU.mult))
                    sv.inc(V.tensor_tensor(e6, pE[0:1, 32:64], a127[0:1, 0:32], ALU.mult))
                    sv.inc(V.tensor_tensor(sml[0:1, 32:64], sml[0:1, 32:64], e6, ALU.add))
                    if not is_halo:
                        sv.wait(T)
                        stn.inc(T.matmul(pF[0:128, 320:384], C["onesrowf"][:, :], carry[:],
                                         start=True, stop=True))
                        stn.wait(V)
                        cr_, ci_ = pF[0:128, 320:352], pF[0:128, 352:384]
                        sv.inc(V.tensor_tensor(t3, cr_, p_pos1[:, 0:32], ALU.mult))
                        sv.inc(V.tensor_tensor(t4, ci_, p_pos1[:, 32:64], ALU.mult))
                        sv.inc(V.tensor_tensor(t3, t3, t4, ALU.subtract))
                        sv.inc(V.tensor_tensor(lre, lre, t3, ALU.add))
                        sv.inc(V.tensor_tensor(t3, cr_, p_pos1[:, 32:64], ALU.mult))
                        sv.inc(V.tensor_tensor(t4, ci_, p_pos1[:, 0:32], ALU.mult))
                        sv.inc(V.tensor_tensor(t3, t3, t4, ALU.add))
                        sv.inc(V.tensor_tensor(lim, lim, t3, ALU.add))
                        sv.inc(V.tensor_scalar(lre, lre, 2.0, None, ALU.mult))
                        sv.inc(V.tensor_scalar(lim, lim, -2.0, None, ALU.mult))
                        sv.wait(T)
                        stn.inc(T.transpose(pF[0:64, 384:512], EV[1][:, 0:64], C["idf"][:, :]))
                        stn.wait(V)
                        sv.inc(V.tensor_copy(xsT[:, 128 * sub:128 * sub + 128], pF[0:64, 384:512]))
                    c5, c6 = sml[0:1, 64:96], sml[0:1, 96:128]
                    c7, c8 = sml[0:1, 128:160], sml[0:1, 160:192]
                    sv.inc(V.tensor_tensor(c5, carry[0:1, 0:32], a128[0:1, 0:32], ALU.mult))
                    sv.inc(V.tensor_tensor(c6, carry[0:1, 32:64], a128[0:1, 32:64], ALU.mult))
                    sv.inc(V.tensor_tensor(c7, c5, c6, ALU.subtract))
                    sv.inc(V.tensor_tensor(c5, carry[0:1, 0:32], a128[0:1, 32:64], ALU.mult))
                    sv.inc(V.tensor_tensor(c6, carry[0:1, 32:64], a128[0:1, 0:32], ALU.mult))
                    sv.inc(V.tensor_tensor(c8, c5, c6, ALU.add))
                    sv.inc(V.tensor_tensor(carry[0:1, 0:32], sml[0:1, 0:32], c7, ALU.add))
                    sv.inc(V.tensor_tensor(carry[0:1, 32:64], sml[0:1, 32:64], c8, ALU.add))

            for ti in range(8):
                is_halo = ti < 4
                src = ins["xh"] if is_halo else ins["xo"]
                tb = (ti % 4) * 512
                ap_in = bass.AP(src, tb, [[HS, 128], [128 * HS, 16], [1, 512]])
                SY.wait_ge(sv.h, sv.v)
                SY.wait_ge(ss.h, ss.v)
                sy.inc(SY.dma_start(out=AX[:], in_=ap_in), 16)
                sy.wait(V)
                sy.wait(SC)
                sy.wait(T)
                emit_ln(AX, AXN)
                if debug and not is_halo:
                    SC.wait_ge(sv.h, sv.v)
                    ss.inc(SC.dma_start(out=bass.AP(xnd, tb, [[HS, 128], [128 * HS, 16], [1, 512]]),
                                        in_=AXN[:]), 16)
                    V.wait_ge(ss.h, ss.v)
                emit_scan(AXN, is_halo)
                if is_halo:
                    continue
                for od in range(16):
                    pf = pB[od % 2]
                    sv.wait(T)
                    for ks in range(16):
                        stn.inc(T.matmul(pf[:, 0:512],
                                         WB[:, 2048 * ks + 128 * od:2048 * ks + 128 * od + 128],
                                         AXN[:, 512 * ks:512 * ks + 512],
                                         start=(ks == 0), stop=(ks == 15)))
                    stn.inc(T.matmul(pA[0][:, 0:512], cTb[:, 128 * od:128 * od + 128],
                                     xsT[:], start=True, stop=True))
                    stn.wait(V)
                    sl = slice(512 * od, 512 * od + 512)
                    sv.inc(V.tensor_scalar(EV[0][:], pf[:, 0:512], scl[:, 0:1],
                                           C["b0col"][:, od:od + 1], ALU.mult, ALU.add))
                    if debug:
                        sv.inc(V.tensor_copy(EV[2][:], pA[0][:, 0:512]))
                        SC.wait_ge(sv.h, sv.v)
                        ss.inc(SC.dma_start(out=bass.AP(f0d, 128 * od * HS + tb, [[HS, 128], [1, 512]]),
                                            in_=EV[0][:]), 16)
                        ss.inc(SC.dma_start(out=bass.AP(y0d, 128 * od * HS + tb, [[HS, 128], [1, 512]]),
                                            in_=EV[2][:]), 16)
                        V.wait_ge(ss.h, ss.v)
                    sv.inc(V.tensor_tensor(EV[1][:], EV[0][:], pA[0][:, 0:512], ALU.add))
                    sv.inc(V.tensor_tensor(EV[1][:], EV[1][:], AX[:, sl], ALU.add))
                    sv.inc(V.tensor_scalar(EV[0][:], AXN[:, sl], C["dcol"][:, od:od + 1],
                                           None, ALU.mult))
                    sv.inc(V.tensor_tensor(AX[:, sl], EV[0][:], EV[1][:], ALU.add))
                emit_ln(AX, AXN)
                ap_x1 = bass.AP(x1s, tb, [[HS, 128], [128 * HS, 16], [1, 512]])
                ap_xn = bass.AP(xn1s, tb, [[HS, 128], [128 * HS, 16], [1, 512]])
                SC.wait_ge(sv.h, sv.v)
                ss.inc(SC.dma_start(out=ap_x1, in_=AX[:]), 16)
                ss.inc(SC.dma_start(out=ap_xn, in_=AXN[:]), 16)
            if debug:
                SC.wait_ge(sv.h, sv.v)
                ss.inc(SC.dma_start(out=dbgrow.ap(), in_=srow[:]), 16)
                ss.inc(SC.dma_start(out=bass.AP(dbgtile, 0, [[1024, 128], [1, 64]]),
                                    in_=p_pos[:]), 16)
                ss.inc(SC.dma_start(out=bass.AP(dbgtile, 64, [[1024, 128], [1, 64]]),
                                    in_=p_neg[:]), 16)
                ss.inc(SC.dma_start(out=bass.AP(dbgtile, 128, [[1024, 128], [1, 4]]),
                                    in_=scl[:]), 16)
                sg.inc(G.dma_start(out=bass.AP(dbgtile, 132, [[1024, 128], [1, 64]]),
                                   in_=wbT[:, 0:64]), 16)
                G.wait_ge(sg.h, sg.v)
            SC.wait_ge(ss.h, ss.v)
            SY.wait_ge(sy.h, sy.v)
            barrier()

        if upto == "p1":
            _tail(nc, V, SY, out, EV[0], sy)
            return nc

        with contextlib.ExitStack() as es2:
            sb2 = lambda n, s, d: es2.enter_context(nc.sbuf_tensor(n, s, d))
            ropecS = sb2("ropecS", [128, HS], F32)
            ropesS = sb2("ropesS", [128, HS], F32)
            shq = sb2("shq", [128, 512], F32)
            nqmax = sb2("nqmax", [1, 512], F32)
            nkmax = sb2("nkmax", [1, 512], F32)
            statsb = sb2("statsb", [1, 8], BF16)

            ap_xn1f = bass.AP(xn1s, 0, [[HS, 128], [128 * HS, 16], [1, 2048]])
            sy.inc(SY.dma_start(out=WB[:], in_=ap_xn1f), 16)
            sy.inc(SY.dma_start(out=ropecS[:], in_=ins["ropec"].ap()), 16)
            sy.inc(SY.dma_start(out=ropesS[:], in_=ins["ropes"].ap()), 16)
            sy.wait(V)
            sy.wait(T)
            sy.wait(G)
            sv.inc(V.memset(nqmax[:], 0.0))
            sv.inc(V.memset(nkmax[:], 0.0))

            def wblock_ap(w, od):
                return bass.AP(w, 128 * od, [[D, 128], [128 * D, 16], [1, 128]])

            PPs = [pA[0], pA[1], pB[0], pB[1]]

            def emit_qk(w, dst, dst_row, nmax):
                for od in range(16):
                    b = od % 2
                    G.wait_ge(stn.h, stn.v)
                    sg.inc(G.dma_start(out=WS[b][:], in_=wblock_ap(w, od)), 16)
                    sg.wait(T)
                    T.wait_ge(sv.h, sv.v)
                    for tc in range(4):
                        for ks in range(16):
                            stn.inc(T.matmul(PPs[tc][:, 0:512], WS[b][:, 128 * ks:128 * ks + 128],
                                             WB[:, 2048 * ks + 512 * tc:2048 * ks + 512 * tc + 512],
                                             start=(ks == 0), stop=(ks == 15)))
                    stn.wait(V)
                    for tc in range(4):
                        tsl = slice(512 * tc, 512 * tc + 512)
                        sv.inc(V.stream_shuffle(shq[:], PPs[tc][:, 0:512], mask=SHUF_SWAP))
                        sv.inc(V.tensor_tensor(EV[0][:], PPs[tc][:, 0:512], ropecS[:, tsl], ALU.mult))
                        sv.inc(V.tensor_tensor(EV[1][:], shq[:], ropesS[:, tsl], ALU.mult))
                        sv.inc(V.tensor_tensor(EVB[:, 512 * tc:512 * tc + 512], EV[0][:], EV[1][:], ALU.add))
                    sv.wait(SC)
                    ssc.inc(SC.activation(WS2[0][:], EVB[:], AF.Square))
                    ssc.wait(T)
                    for tc in range(4):
                        T.wait_ge(sv.h, sv.v)
                        stn.inc(T.matmul(pC[0:1, 0:512], C["onescolb"][:],
                                         WS2[0][:, 512 * tc:512 * tc + 512], start=True, stop=True))
                        stn.wait(V)
                        sv.inc(V.tensor_tensor(nmax[:], nmax[:], pC[0:1, 0:512], ALU.max))
                    SC.wait_ge(sv.h, sv.v)
                    ss.inc(SC.dma_start(out=bass.AP(dst, dst_row * D + 128 * od * D,
                                                    [[D, 128], [1, 2048]]),
                                        in_=EVB[:]), 16)

            emit_qk(ins["wq"], qscr, 0, nqmax)
            emit_qk(ins["wk"], cc_kv_i, 0, nkmax)

            for od in range(16):
                b = od % 2
                G.wait_ge(stn.h, stn.v)
                sg.inc(G.dma_start(out=WS[b][:], in_=wblock_ap(ins["wv"], od)), 16)
                sg.wait(T)
                T.wait_ge(sv.h, sv.v)
                for tc in range(4):
                    for ks in range(16):
                        stn.inc(T.matmul(PPs[tc][:, 0:512], WS[b][:, 128 * ks:128 * ks + 128],
                                         WB[:, 2048 * ks + 512 * tc:2048 * ks + 512 * tc + 512],
                                         start=(ks == 0), stop=(ks == 15)))
                stn.wait(V)
                V.wait_ge(ss.h, ss.v)
                for tc in range(4):
                    sv.inc(V.tensor_copy(EV[2][:], PPs[tc][:, 0:512]))
                    for j in range(4):
                        tb_ = 4 * tc + j
                        pq = pC if tb_ % 2 == 0 else pD
                        sv.wait(T)
                        stn.inc(T.transpose(pq[0:128, 0:128],
                                            EV[2][:, 128 * j:128 * j + 128], C["idf"][:, :]))
                        stn.wait(V)
                        sv.inc(V.tensor_copy(WS2[1][:, 128 * tb_:128 * tb_ + 128], pq[0:128, 0:128]))
                SC.wait_ge(sv.h, sv.v)
                ss.inc(SC.dma_start(
                    out=bass.AP(cc_kv_i, 2048 * D + 128 * od, [[D, 128], [128 * D, 16], [1, 128]]),
                    in_=bass.AP(WS2[1], 0, [[2048, 128], [128, 16], [1, 128]])), 16)


            sv.inc(V.tensor_reduce(sml[0:1, 0:1], nqmax[:], AXL, ALU.max))
            sv.inc(V.tensor_reduce(sml[0:1, 1:2], nkmax[:], AXL, ALU.max))
            sv.inc(V.tensor_copy(statsb[0:1, 0:2], sml[0:1, 0:2]))
            SC.wait_ge(sv.h, sv.v)
            ss.inc(SC.dma_start(out=bass.AP(cc_kv_i, 4096 * D, [[D, 1], [1, 8]]),
                                in_=statsb[:]), 16)
            SC.wait_ge(ss.h, ss.v)
            SY.wait_ge(sy.h, sy.v)
            barrier()

            scc.inc(G.collective_compute(
                "AllGather", ALU.bypass,
                replica_groups=[[0, 1], [2, 3], [4, 5], [6, 7]],
                ins=[cc_kv_i.ap()], outs=[cc_kv_o.ap()]))
            scc.wait(G)
            if debug:
                sg.inc(G.dma_start(out=ckvo_d.ap(), in_=cc_kv_o.ap()), 16)
                G.wait_ge(sg.h, sg.v)
            barrier()

        if upto == "p2":
            _tail(nc, V, SY, out, EV[0], sy)
            return nc

        with contextlib.ExitStack() as es3:
            sb3 = lambda n, s, d: es3.enter_context(nc.sbuf_tensor(n, s, d))
            KH = sb3("KH", [128, S], BF16)
            VH = sb3("VH", [128, S], BF16)
            QH = sb3("QH", [128, HS], BF16)
            PBu = sb3("PBu", [128, S], BF16)
            P2B = [sb3(f"P2B{i}", [128, 512], BF16) for i in range(2)]
            mscr = sb3("mscr", [128, 512], BF16)
            tmpev = sb3("tmpev", [128, 1024], F32)
            Tall = sb3("Tall", [128, 512], F32)
            Tprt = sb3("Tprt", [128, 512], F32)
            bis = sb3("bis", [16, 8], F32)
            mid128 = sb3("mid128", [128, 1], F32)
            cnt128 = sb3("cnt128", [128, 1], F32)
            hvt = sb3("hvt", [128, 512], F32)
            hb01 = [sb3(f"hb01_{j}", [128, 128], F32) for j in range(4)]
            hbias = [sb3(f"hbias{j}", [128, 128], F32) for j in range(4)]
            zrec = sb3("zrec", [128, 512], F32)
            zbc = sb3("zbc", [128, 512], F32)

            sg.inc(G.dma_start(out=sml[0:1, 0:2],
                               in_=bass.AP(cc_kv_o, 4096 * D, [[D, 1], [1, 2]])), 16)
            sg.inc(G.dma_start(out=sml[0:1, 2:4],
                               in_=bass.AP(cc_kv_o, 8320 * D, [[D, 1], [1, 2]])), 16)
            sg.wait(V)
            sv.inc(V.tensor_tensor(sml[0:1, 4:5], sml[0:1, 0:1], sml[0:1, 2:3], ALU.max))
            sv.inc(V.tensor_tensor(sml[0:1, 5:6], sml[0:1, 1:2], sml[0:1, 3:4], ALU.max))
            sv.inc(V.tensor_tensor(sml[0:1, 6:7], sml[0:1, 4:5], sml[0:1, 5:6], ALU.add))
            sv.inc(V.tensor_scalar(sml[0:1, 6:7], sml[0:1, 6:7], float(ISD / 2), 2.0,
                                   ALU.mult, ALU.add))
            sv.inc(V.tensor_scalar(sml[0:1, 7:8], sml[0:1, 6:7], -1.0, None, ALU.mult))
            sv.wait(T)
            stn.inc(T.matmul(pF[0:128, 0:1], C["onesrowf"][:, :], sml[0:1, 7:8],
                             start=True, stop=True))
            stn.wait(V)
            sv.inc(V.tensor_copy(negc[:], pF[0:128, 0:1]))
            sv.inc(V.tensor_scalar(negcB[:], negc[:], 1.0, -80.0, ALU.mult, ALU.add))
            barrier()

            def rec_sweep1(qbase):
                q = {"v": [], "t": [], "s": [], "y": []}
                for h in range(NH):
                    stn.rwait(q, "y", SY)
                    ssc.rwait(q, "y", SY)
                    sy.rinc(q, "y", SY.dma_start, (), 16,
                            dict(out=QH[:], in_=bass.AP(qscr, 128 * h * HS, [[HS, 128], [1, HS]])))
                    sy.rinc(q, "y", SY.dma_start, (), 16,
                            dict(out=KH[:, 0:2048],
                                 in_=bass.AP(cc_kv_o, 128 * h * D, [[D, 128], [1, 2048]])))
                    if qbase == 2048:
                        sy.rinc(q, "y", SY.dma_start, (), 16,
                                dict(out=KH[:, 2048:4096],
                                     in_=bass.AP(cc_kv_o, (4224 + 128 * h) * D, [[D, 128], [1, 2048]])))
                    sy.rwait(q, "t", T)
                    sy.rwait(q, "v", V)
                    stn.rwait(q, "v", V)
                    sv.rinc(q, "v", V.memset, (pC[:, 0:512], 0.0))
                    sv.rinc(q, "v", V.memset, (pD[:, 0:512], 0.0))
                    for qt in range(16):
                        qg = qbase + 128 * qt
                        kmax = qg + 128
                        stat = QH[:, 128 * qt:128 * qt + 128]
                        pieces = []
                        k0 = 0
                        while k0 < kmax:
                            kw = min(512, kmax - k0)
                            pieces.append((k0, kw))
                            k0 += kw
                        for pi_, (k0, kw) in enumerate(pieces):
                            pp = pA[pi_ % 2]
                            sv.rwait(q, "t", T)
                            ssc.rwait(q, "t", T)
                            stn.rinc(q, "t", T.matmul, (pp[:, 0:kw], stat, KH[:, k0:k0 + kw]),
                                     1, dict(start=True, stop=True))
                            last = (pi_ == len(pieces) - 1)
                            if last:
                                stn.rwait(q, "v", V)
                                sv.rinc(q, "v", V.tensor_tensor,
                                        (pp[:, kw - 128:kw], pp[:, kw - 128:kw], C["dkill"][:], ALU.add))
                                sv.rwait(q, "s", SC)
                            stn.rwait(q, "s", SC)
                            sv.rwait(q, "s", SC)
                            ssc.rinc(q, "s", SC.activation, (PBu[:, k0:k0 + kw], pp[:, 0:kw], AF.Exp),
                                     1, dict(bias=negc[:, 0:1], scale=ISD,
                                             accum_out=(zacc[:] if pi_ == 0 else zprt[:])))
                            if pi_ > 0:
                                ssc.rwait(q, "v", V)
                                sv.rinc(q, "v", V.tensor_tensor, (zacc[:], zacc[:], zprt[:], ALU.add))
                        ssc.rwait(q, "v", V)
                        sv.rinc(q, "v", V.reciprocal, (invz[:], zacc[:]))
                        sv.rinc(q, "v", V.tensor_scalar,
                                (PBu[:, 0:kmax], PBu[:, 0:kmax], invz[:, 0:1], None, ALU.mult))
                        sv.rwait(q, "t", T)
                        nch = (kmax + 511) // 512
                        for j in range(nch):
                            w = min(512, kmax - 512 * j)
                            bank = pC if j < 4 else pD
                            part = 32 * (j % 4)
                            stn.rinc(q, "t", T.matmul,
                                     (bank[part:part + 1, 0:w], C["onescolb"][:],
                                      PBu[:, 512 * j:512 * j + w]),
                                     1, dict(start=False, stop=True, tile_position=(0, part)))
                    stn.rwait(q, "v", V)
                    ss.rwait(q, "v", V)
                    sv.rinc(q, "v", V.tensor_copy, (tmpev[:, 0:512], pC[:, 0:512]))
                    sv.rinc(q, "v", V.tensor_copy, (tmpev[:, 512:1024], pD[:, 0:512]))
                    sv.rwait(q, "s", SC)
                    src_ap = bass.AP(tmpev, 0, [[32 * 1024, 4], [512, 2], [1, 512]])
                    dst_ap = bass.AP(cc_t_i, h * S, [[512, 4], [2048, 2], [1, 512]])
                    ss.rinc(q, "s", SC.dma_start, (), 16, dict(out=dst_ap, in_=src_ap))
                ss.rwait(q, "s", SC)
                sy.rwait(q, "y", SY)
                return q

            p3sems = [sy, sg, ss, sv, stn, ssc]
            snap = {id(s): s.v for s in p3sems}
            q_up = rec_sweep1(2048)
            up_v = {id(s): s.v for s in p3sems}
            for s in p3sems:
                s.v = snap[id(s)]
            q_lo = rec_sweep1(0)
            lo_v = {id(s): s.v for s in p3sems}
            pads = {}
            for s in p3sems:
                d_ = up_v[id(s)] - lo_v[id(s)]
                assert d_ >= 0, "lower branch exceeds upper on a semaphore"
                pads[id(s)] = d_
                s.v = up_v[id(s)]
            for key, e in (("t", T), ("s", SC), ("v", V), ("y", SY)):
                with e.If_eq(rreg[id(e)], 1):
                    _replay(q_up, key)
                with e.Else():
                    _replay(q_lo, key)
                    if key == "v":
                        for s in p3sems:
                            if pads[id(s)]:
                                V.sem_inc(s.h, pads[id(s)])
                e.end_ifs()
            barrier()

            scc.inc(G.collective_compute(
                "AllGather", ALU.bypass,
                replica_groups=[[0, 1], [2, 3], [4, 5], [6, 7]],
                ins=[cc_t_i.ap()], outs=[cc_t_o.ap()]))
            scc.wait(G)
            if debug:
                sg.inc(G.dma_start(out=cto_d.ap(), in_=cc_t_o.ap()), 16)
                G.wait_ge(sg.h, sg.v)
            barrier()
            sy.inc(SY.dma_start(out=Tall[:], in_=bass.AP(cc_t_o, 0, [[512, 128], [1, 512]])), 16)
            sy.inc(SY.dma_start(out=Tprt[:], in_=bass.AP(cc_t_o, 16 * S, [[512, 128], [1, 512]])), 16)
            sy.wait(V)
            sv.inc(V.tensor_tensor(Tall[:], Tall[:], Tprt[:], ALU.add))
            sv.inc(V.memset(bis[:, 0:1], 0.0))
            sv.inc(V.memset(bis[:, 1:2], 4096.0))
            for it in range(NBIS):
                sv.inc(V.tensor_tensor(bis[:, 2:3], bis[:, 0:1], bis[:, 1:2], ALU.add))
                sv.inc(V.tensor_scalar(bis[:, 2:3], bis[:, 2:3], 0.5, None, ALU.mult))
                sv.wait(T)
                stn.inc(T.matmul(pF[0:128, 0:1], C["e8t"][:, :], bis[:, 2:3], start=True, stop=True))
                stn.wait(V)
                sv.inc(V.tensor_copy(mid128[:], pF[0:128, 0:1]))
                sv.inc(V.tensor_scalar(hvt[:], Tall[:], mid128[:, 0:1], 0.0, ALU.is_gt,
                                       ALU.add, accum_out=cnt128[:]))
                sv.wait(T)
                stn.inc(T.matmul(pF[0:16, 4:5], C["e8"][:, :], cnt128[:], start=True, stop=True))
                stn.wait(V)
                sv.inc(V.tensor_scalar(bis[:, 3:4], pF[0:16, 4:5], 408.5, None, ALU.is_ge))
                sv.inc(V.tensor_scalar(bis[:, 4:5], bis[:, 3:4], -1.0, 1.0, ALU.mult, ALU.add))
                sv.inc(V.tensor_tensor(bis[:, 5:6], bis[:, 2:3], bis[:, 0:1], ALU.subtract))
                sv.inc(V.tensor_tensor(bis[:, 5:6], bis[:, 5:6], bis[:, 3:4], ALU.mult))
                sv.inc(V.tensor_tensor(bis[:, 0:1], bis[:, 0:1], bis[:, 5:6], ALU.add))
                sv.inc(V.tensor_tensor(bis[:, 5:6], bis[:, 2:3], bis[:, 1:2], ALU.subtract))
                sv.inc(V.tensor_tensor(bis[:, 5:6], bis[:, 5:6], bis[:, 4:5], ALU.mult))
                sv.inc(V.tensor_tensor(bis[:, 1:2], bis[:, 1:2], bis[:, 5:6], ALU.add))
            sv.wait(T)
            stn.inc(T.matmul(pF[0:128, 0:1], C["e8t"][:, :], bis[:, 0:1], start=True, stop=True))
            stn.wait(V)
            sv.inc(V.tensor_copy(mid128[:], pF[0:128, 0:1]))
            sv.inc(V.tensor_scalar(hvt[:], Tall[:], mid128[:, 0:1], None, ALU.is_gt))
            sv.wait(T)
            for j in range(4):
                stn.inc(T.transpose(pF[0:128, 0:128], hvt[:, 128 * j:128 * j + 128], C["idf"][:, :]))
                stn.wait(V)
                sv.inc(V.tensor_copy(hb01[j][:], pF[0:128, 0:128]))
                sv.inc(V.tensor_scalar(hbias[j][:], hb01[j][:], 80.0, negcB[:, 0:1],
                                       ALU.mult, ALU.add))
                sv.wait(T)
            barrier()

            def rec_sweep2(qbase):
                q = {"v": [], "t": [], "s": [], "y": []}
                ktmax = 16 + (16 if qbase == 2048 else 0)
                for h in range(NH):
                    stn.rwait(q, "y", SY)
                    ssc.rwait(q, "y", SY)
                    sy.rinc(q, "y", SY.dma_start, (), 16,
                            dict(out=QH[:], in_=bass.AP(qscr, 128 * h * HS, [[HS, 128], [1, HS]])))
                    sy.rinc(q, "y", SY.dma_start, (), 16,
                            dict(out=KH[:, 0:2048],
                                 in_=bass.AP(cc_kv_o, 128 * h * D, [[D, 128], [1, 2048]])))
                    sy.rinc(q, "y", SY.dma_start, (), 16,
                            dict(out=bass.AP(VH, 0, [[S, 128], [128, 16], [1, 128]]),
                                 in_=bass.AP(cc_kv_o, 2048 * D + 128 * h,
                                             [[D, 128], [128 * D, 16], [1, 128]])))
                    if qbase == 2048:
                        sy.rinc(q, "y", SY.dma_start, (), 16,
                                dict(out=KH[:, 2048:4096],
                                     in_=bass.AP(cc_kv_o, (4224 + 128 * h) * D, [[D, 128], [1, 2048]])))
                        sy.rinc(q, "y", SY.dma_start, (), 16,
                                dict(out=bass.AP(VH, 2048, [[S, 128], [128, 16], [1, 128]]),
                                     in_=bass.AP(cc_kv_o, (4224 + 2048) * D + 128 * h,
                                                 [[D, 128], [128 * D, 16], [1, 128]])))
                    sy.rwait(q, "t", T)
                    sy.rwait(q, "v", V)
                    stn.rwait(q, "v", V)
                    for pv in (pB[0], pB[1], pE, pF):
                        sv.rinc(q, "v", V.memset, (pv[:, 0:512], 0.0))
                    sv.rinc(q, "v", V.memset, (pC[:, 0:512], 0.0))
                    pcnt = 0
                    for kt in range(ktmax):
                        kg = 128 * kt
                        statK = KH[:, kg:kg + 128]
                        statV = VH[:, kg:kg + 128]
                        qlo = max(0, kg - qbase)
                        for pc in range(qlo // 512, 4):
                            q0 = 512 * pc
                            dlt = qbase + q0 - kg
                            amin, amax = dlt - 127, dlt + 511
                            al = max(0, kg - qbase - q0)
                            pure_h = amin > BAND
                            pp = pA[pcnt % 2]
                            pbuf = P2B[pcnt % 2]
                            sv.rwait(q, "t", T)
                            ssc.rwait(q, "t", T)
                            stn.rinc(q, "t", T.matmul, (pp[:, 0:512], statK, QH[:, q0:q0 + 512]),
                                     1, dict(start=True, stop=True))
                            bias = hbias[kt % 4][:, 8 * h + kt // 4:8 * h + kt // 4 + 1] \
                                if pure_h else negc[:, 0:1]
                            stn.rwait(q, "s", SC)
                            sv.rwait(q, "s", SC)
                            ssc.rinc(q, "s", SC.activation, (pbuf[:], pp[:, 0:512], AF.Exp),
                                     1, dict(bias=bias, scale=ISD))
                            need_mask = not pure_h and not (amin >= 0 and amax <= BAND)
                            if need_mask:
                                ssc.rwait(q, "v", V)
                                hcol = hb01[kt % 4][:, 8 * h + kt // 4:8 * h + kt // 4 + 1]
                                if amin >= 0 and amax > BAND:
                                    sv.rinc(q, "v", V.tensor_scalar,
                                            (mscr[:], C[f"bc_{dlt}"][:], hcol, None, ALU.max))
                                    sv.rinc(q, "v", V.tensor_tensor,
                                            (pbuf[:], pbuf[:], mscr[:], ALU.mult))
                                elif amax <= BAND:
                                    sv.rinc(q, "v", V.tensor_tensor,
                                            (pbuf[:], pbuf[:], C[f"bc_{dlt}"][:], ALU.mult))
                                else:
                                    sv.rinc(q, "v", V.tensor_scalar,
                                            (mscr[:], C["cz_0"][:], hcol, None, ALU.mult))
                                    sv.rinc(q, "v", V.tensor_tensor,
                                            (mscr[:], mscr[:], C[f"bc_{dlt}"][:], ALU.max))
                                    sv.rinc(q, "v", V.tensor_tensor,
                                            (pbuf[:], pbuf[:], mscr[:], ALU.mult))
                                sv.rwait(q, "t", T)
                            else:
                                ssc.rwait(q, "t", T)
                            pvp = [pB[0], pB[1], pE, pF][pc]
                            stn.rinc(q, "t", T.matmul,
                                     (pvp[:, al:512], statV, pbuf[:, al:512]),
                                     1, dict(start=False, stop=True))
                            stn.rinc(q, "t", T.matmul,
                                     (pC[32 * pc:32 * pc + 1, al:512], C["onescolb"][:],
                                      pbuf[:, al:512]),
                                     1, dict(start=False, stop=True, tile_position=(0, 32 * pc)))
                            pcnt += 1
                    stn.rwait(q, "v", V)
                    sv.rinc(q, "v", V.reciprocal, (zrec[:], pC[:, 0:512]))
                    for pc in range(4):
                        pvp = [pB[0], pB[1], pE, pF][pc]
                        sv.rwait(q, "t", T)
                        stn.rinc(q, "t", T.matmul,
                                 (pD[0:128, 0:512], onesall[32 * pc:32 * pc + 1, 0:128],
                                  zrec[32 * pc:32 * pc + 1, 0:512]),
                                 1, dict(start=True, stop=True, tile_position=(32 * pc, 0)))
                        stn.rwait(q, "v", V)
                        sv.rinc(q, "v", V.tensor_copy, (zbc[:], pD[0:128, 0:512]))
                        sv.rinc(q, "v", V.tensor_tensor,
                                (WB[:, 2048 * h + 512 * pc:2048 * h + 512 * pc + 512],
                                 pvp[:, 0:512], zbc[:], ALU.mult))
                sy.rwait(q, "y", SY)
                return q

            snap = {id(s): s.v for s in p3sems}
            q_up = rec_sweep2(2048)
            up_v = {id(s): s.v for s in p3sems}
            for s in p3sems:
                s.v = snap[id(s)]
            q_lo = rec_sweep2(0)
            lo_v = {id(s): s.v for s in p3sems}
            pads = {}
            for s in p3sems:
                d_ = up_v[id(s)] - lo_v[id(s)]
                assert d_ >= 0, "lower branch exceeds upper on a semaphore (sweep2)"
                pads[id(s)] = d_
                s.v = up_v[id(s)]
            for key, e in (("t", T), ("s", SC), ("v", V), ("y", SY)):
                with e.If_eq(rreg[id(e)], 1):
                    _replay(q_up, key)
                with e.Else():
                    _replay(q_lo, key)
                    if key == "v":
                        for s in p3sems:
                            if pads[id(s)]:
                                V.sem_inc(s.h, pads[id(s)])
                e.end_ifs()
            barrier()
            if debug:
                for h in range(16):
                    ss.inc(SC.dma_start(out=bass.AP(attnd, 128 * h * HS, [[HS, 128], [1, HS]]),
                                        in_=WB[:, 2048 * h:2048 * h + 2048]), 16)
                SC.wait_ge(ss.h, ss.v)
                barrier()

        if upto == "p3":
            _tail(nc, V, SY, out, EV[0], sy)
            return nc

        with contextlib.ExitStack() as es4:
            sb4 = lambda n, s, d: es4.enter_context(nc.sbuf_tensor(n, s, d))
            AXN4 = sb4("AXN4", [128, 16 * 512], BF16)
            rw1b = sb4("rw1b", [128, 16 * 64], BF16)
            rw2b = sb4("rw2b", [64, 16], BF16)
            atb = sb4("atb", [128, 16 * 256], BF16)
            btb = [sb4(f"btb{i}", [128, 2048], BF16) for i in range(2)]
            rb2gb = sb4("rb2gb", [128, 16], F32)
            hTf = sb4("hTf", [64, 512], F32)
            hTb = sb4("hTb", [64, 512], BF16)
            coefT = sb4("coefT", [16, 512], F32)
            cexp = sb4("cexp", [128, 512], F32)
            wxa = sb4("wxa", [128, 2 * 2048], BF16)
            lgt = sb4("lgt", [128, 16], F32)
            lgt2 = sb4("lgt2", [128, 16], F32)
            lsel = sb4("lsel", [128, 16], F32)
            lcol = [sb4(f"lcol{i}", [128, 1], F32) for i in range(3)]
            X1B = sb4("X1B", [128, 2048], F32)
            BSB = sb4("BSB", [128, 2048], F32)

            sg.inc(G.dma_start(out=rw1b[:], in_=bass.AP(ins["rw1"], 0,
                                                        [[64, 128], [128 * 64, 16], [1, 64]])), 16)
            sg.inc(G.dma_start(out=rw2b[:], in_=ins["rw2"].ap()), 16)
            sg.inc(G.dma_start(out=atb[:], in_=bass.AP(ins["at"], 0,
                                                       [[256, 128], [128 * 256, 16], [1, 256]])), 16)
            for i in range(2):
                sg.inc(G.dma_start(out=btb[i][:], in_=ins["bt"].ap()[128 * i:128 * i + 128, :]), 16)
            sg.wait(V)
            sv.inc(V.tensor_tensor(sml[0:1, 0:16], srow[0:1, 160:176], srow[0:1, 176:192], ALU.add))
            sv.wait(T)
            stn.inc(T.matmul(pF[0:128, 0:16], C["onesrowf"][:, :], sml[0:1, 0:16],
                             start=True, stop=True))
            stn.wait(V)
            sv.inc(V.tensor_copy(rb2gb[:], pF[0:128, 0:16]))

            for tc in range(4):
                ap_xn = bass.AP(xn1s, 512 * tc, [[HS, 128], [128 * HS, 16], [1, 512]])
                SY.wait_ge(sv.h, sv.v)
                sy.inc(SY.dma_start(out=AXN4[:], in_=ap_xn), 16)
                sy.wait(T)
                sy.wait(V)
                sv.wait(T)
                ssc.wait(T)
                for ks in range(16):
                    stn.inc(T.matmul(pC[0:64, 0:512], rw1b[:, 64 * ks:64 * ks + 64],
                                     AXN4[:, 512 * ks:512 * ks + 512],
                                     start=(ks == 0), stop=(ks == 15)))
                stn.wait(SC)
                ssc.inc(SC.activation(hTf[:], pC[0:64, 0:512], AF.Exp, scale=-1.0))
                ssc.wait(V)
                sv.inc(V.tensor_scalar(hTf[:], hTf[:], 1.0, None, ALU.add))
                sv.inc(V.reciprocal(hTf[:], hTf[:]))
                sv.inc(V.tensor_tensor(hTb[:], pC[0:64, 0:512], hTf[:], ALU.mult))
                for tsub in range(4):
                    sv.wait(T)
                    stn.inc(T.matmul(pD[0:128, 0:16], hTb[:, 128 * tsub:128 * tsub + 128],
                                     rw2b[:], start=True, stop=True))
                    stn.wait(V)
                    sv.inc(V.tensor_tensor(lgt[:], pD[0:128, 0:16], rb2gb[:], ALU.add))
                    sv.inc(V.reduce_max(lcol[0][:], lgt[:], AXL))
                    sv.inc(V.tensor_scalar(lgt2[:], lgt[:], lcol[0][:, 0:1], None, ALU.subtract))
                    sv.inc(V.tensor_scalar(lsel[:], lgt2[:], 0.0, None, ALU.is_ge))
                    sv.inc(V.tensor_scalar(lgt[:], lsel[:], -1e30, None, ALU.mult))
                    sv.inc(V.tensor_tensor(lgt[:], lgt[:], lgt2[:], ALU.add))
                    sv.inc(V.reduce_max(lcol[1][:], lgt[:], AXL))
                    sv.inc(V.tensor_scalar(lsel[:], lgt2[:], lcol[1][:, 0:1], None, ALU.is_ge))
                    sv.wait(SC)
                    ssc.inc(SC.activation(lgt[:], lgt2[:], AF.Exp))
                    ssc.inc(SC.activation(lcol[2][:], lcol[1][:], AF.Exp))
                    ssc.wait(V)
                    sv.inc(V.tensor_scalar(lcol[2][:], lcol[2][:], 1.0, None, ALU.add))
                    sv.inc(V.reciprocal(lcol[2][:], lcol[2][:]))
                    sv.inc(V.tensor_tensor(lgt[:], lgt[:], lsel[:], ALU.mult))
                    sv.inc(V.tensor_scalar(lgt2[:], lgt[:], lcol[2][:, 0:1], None, ALU.mult))
                    sv.wait(T)
                    stn.inc(T.transpose(pE[0:16, 0:128], lgt2[:], C["idf"][:, :]))
                    stn.wait(V)
                    sv.inc(V.tensor_copy(coefT[:, 128 * tsub:128 * tsub + 128], pE[0:16, 0:128]))
                for i in range(2):
                    sv.wait(T)
                    stn.inc(T.matmul(pD[0:128, 0:512], C["rep16"][:, 128 * i:128 * i + 128],
                                     coefT[:], start=True, stop=True))
                    for ks in range(16):
                        stn.inc(T.matmul(pB[i][:, 0:512],
                                         atb[:, 256 * ks + 128 * i:256 * ks + 128 * i + 128],
                                         AXN4[:, 512 * ks:512 * ks + 512],
                                         start=(ks == 0), stop=(ks == 15)))
                    stn.wait(V)
                    sv.inc(V.tensor_copy(cexp[:], pD[0:128, 0:512]))
                    sv.inc(V.tensor_tensor(wxa[:, 2048 * i + 512 * tc:2048 * i + 512 * tc + 512],
                                           pB[i][:, 0:512], cexp[:], ALU.mult))
                for od in range(16):
                    b = od % 2
                    G.wait_ge(stn.h, stn.v)
                    sg.inc(G.dma_start(out=WS[b][:], in_=wblock_ap2(ins["w1"], od)), 16)
                    sg.wait(T)
                    T.wait_ge(sv.h, sv.v)
                    pp = pA[od % 2]
                    for ks in range(16):
                        stn.inc(T.matmul(pp[:, 0:512], WS[b][:, 128 * ks:128 * ks + 128],
                                         AXN4[:, 512 * ks:512 * ks + 512],
                                         start=(ks == 0), stop=False))
                    for i in range(2):
                        stn.inc(T.matmul(pp[:, 0:512], btb[i][:, 128 * od:128 * od + 128],
                                         wxa[:, 2048 * i + 512 * tc:2048 * i + 512 * tc + 512],
                                         start=False, stop=(i == 1)))
                    stn.wait(V)
                    V.wait_ge(ss.h, ss.v)
                    sv.inc(V.tensor_scalar(EV[2][:], pp[:, 0:512], 1.0,
                                           C["b1col"][:, od:od + 1], ALU.mult, ALU.add))
                    SC.wait_ge(sv.h, sv.v)
                    ss.inc(SC.dma_start(out=bass.AP(bscrd, 128 * od * HS + 512 * tc,
                                                    [[HS, 128], [1, 512]]),
                                        in_=EV[2][:]), 16)
            SC.wait_ge(ss.h, ss.v)
            barrier()

            for od in range(16):
                b = od % 2
                G.wait_ge(stn.h, stn.v)
                sg.inc(G.dma_start(out=WS2[b][:], in_=wblock_ap2(ins["wo"], od)), 16)
                SY.wait_ge(sv.h, sv.v)
                sy.inc(SY.dma_start(out=X1B[:], in_=x1s.ap()[128 * od:128 * od + 128, :]), 16)
                sy.inc(SY.dma_start(out=BSB[:], in_=bscrd.ap()[128 * od:128 * od + 128, :]), 16)
                sg.wait(T)
                T.wait_ge(sv.h, sv.v)
                for tc in range(4):
                    pp = pA[tc % 2]
                    T.wait_ge(sv.h, sv.v)
                    for hh in range(16):
                        stn.inc(T.matmul(pp[:, 0:512], WS2[b][:, 128 * hh:128 * hh + 128],
                                         WB[:, 2048 * hh + 512 * tc:2048 * hh + 512 * tc + 512],
                                         start=(hh == 0), stop=(hh == 15)))
                    stn.wait(V)
                    sy.wait(V)
                    V.wait_ge(ss.h, ss.v)
                    sv.inc(V.tensor_tensor(EV[2][:], pp[:, 0:512],
                                           X1B[:, 512 * tc:512 * tc + 512], ALU.add))
                    sv.inc(V.tensor_tensor(EV[2][:], EV[2][:],
                                           BSB[:, 512 * tc:512 * tc + 512], ALU.add))
                    SC.wait_ge(sv.h, sv.v)
                    ss.inc(SC.dma_start(out=bass.AP(out, 128 * od * HS + 512 * tc,
                                                    [[HS, 128], [1, 512]]),
                                        in_=EV[2][:]), 16)
            SC.wait_ge(ss.h, ss.v)
            barrier()
    return nc


def wblock_ap2(w, od):
    return bass.AP(w, 128 * od, [[D, 128], [128 * D, 16], [1, 128]])


def _tail(nc, V, SY, out, buf, sy):
    V.memset(buf[:], 0.0)
    V.engine_nop().then_inc(sy.h, 1)
    sy.v += 1
    SY.wait_ge(sy.h, sy.v)
    for od in range(16):
        for tc in range(4):
            SY.dma_start(out=bass.AP(out, 128 * od * HS + 512 * tc,
                                     [[HS, 128], [1, 512]]), in_=buf[:]).then_inc(sy.h, 16)
            sy.v += 16
    SY.wait_ge(sy.h, sy.v)


def _prep_inputs(inputs):
    perm = _perm_heads()
    cst = _host_consts()
    f32 = lambda a: np.ascontiguousarray(np.asarray(a, np.float32))
    wq = f32(np.asarray(inputs["L1_attn_Wq"], np.float32).T[:, perm])
    wk = f32(np.asarray(inputs["L1_attn_Wk"], np.float32).T[:, perm])
    wv = f32(np.asarray(inputs["L1_attn_Wv"], np.float32).T)
    wo = f32(np.asarray(inputs["L1_attn_Wo"], np.float32).T)
    w0 = f32(np.asarray(inputs["L0_ffn_weight"], np.float32).T)
    w1 = f32(np.asarray(inputs["L1_ffn_W"], np.float32).T)
    rw1 = f32(np.asarray(inputs["L1_ffn_rw1"], np.float32).T)
    rw2 = f32(np.asarray(inputs["L1_ffn_rw2"], np.float32).T[:, :16])
    at = f32(np.asarray(inputs["L1_ffn_A"], np.float32).reshape(256, D).T)
    bt = f32(np.asarray(inputs["L1_ffn_B"], np.float32).transpose(1, 0, 2).reshape(D, 256).T)
    cT = f32(np.concatenate([np.asarray(inputs["L0_attn_C_re"], np.float32).T,
                             np.asarray(inputs["L0_attn_C_im"], np.float32).T], 0))
    bT = f32(np.concatenate([np.asarray(inputs["L0_attn_B_re"], np.float32).T,
                             np.asarray(inputs["L0_attn_B_im"], np.float32).T], 1))
    dcol = f32(np.asarray(inputs["L0_attn_D"], np.float32).reshape(16, 128).T)
    b0col = f32(np.asarray(inputs["L0_ffn_bias"], np.float32).reshape(16, 128).T)
    b1col = f32(np.asarray(inputs["L1_ffn_bias"], np.float32).reshape(16, 128).T)
    shared = dict(
        wq=wq, wk=wk, wv=wv, wo=wo, w0=w0, w1=w1, rw1=rw1, rw2=rw2, at=at, bt=bt,
        cT=cT, bT=bT, dcol=dcol, b0col=b0col, b1col=b1col,
        lamur=f32(inputs["L0_attn_Lambda_unconstrained_re"]).reshape(1, 32),
        lamim=f32(inputs["L0_attn_Lambda_im"]).reshape(1, 32),
        logd=f32(np.asarray(inputs["L0_attn_log_Delta"], np.float32)[:32]).reshape(1, 32),
        rb1r=f32(inputs["L1_ffn_rb1"]).reshape(1, 64),
        rb2r=f32(np.asarray(inputs["L1_ffn_rb2"], np.float32)[:16]).reshape(1, 16),
        gater=f32(inputs["L1_ffn_gates"]).reshape(1, 16),
        **cst,
    )
    x = np.asarray(inputs["L0_attn_u"], np.float32)
    in_maps = []
    ropes = [_rope_tables(0), _rope_tables(HS)]
    for c in range(8):
        b_, hf = c // 2, c % 2
        xT = np.ascontiguousarray(x[b_].T)
        m = dict(shared)
        m["xo"] = np.ascontiguousarray(xT[:, HS * hf:HS * (hf + 1)])
        m["xh"] = (np.zeros((D, HS), np.float32) if hf == 0
                   else np.ascontiguousarray(xT[:, 0:HS]))
        m["role"] = np.array([[hf]], np.int32)
        m["ropec"], m["ropes"] = ropes[hf]
        in_maps.append(m)
    return in_maps


def kernel(**inputs):
    in_maps = _prep_inputs(inputs)
    nc = build_nc(_UPTO, _DEBUG)
    res = run_bass_kernel_spmd(nc, in_maps, core_ids=list(range(8)))
    out = np.empty((4, S, D), np.float32)
    for c in range(8):
        b_, hf = c // 2, c % 2
        out[b_, HS * hf:HS * (hf + 1), :] = res.results[c]["out"].T
    return out


# version 02
# speedup vs baseline: 1.1900x; 1.1900x over previous
import contextlib
import numpy as np
import ml_dtypes
import concourse.bass as bass
import concourse.mybir as mybir
from concourse.bass_utils import run_bass_kernel_spmd

F32 = mybir.dt.float32
BF16 = mybir.dt.bfloat16
I32 = mybir.dt.int32
AF = mybir.ActivationFunctionType
ALU = mybir.AluOpType
AXL = mybir.AxisListType.X
BF = ml_dtypes.bfloat16

D = 2048
S = 4096
HS = 2048
NH = 16
DH = 128
BAND = 409
ISD = float(1.0 / np.sqrt(DH))
BIGF = 1.0e30
EPS_LN = 1e-5
NBIS = 40

_UPTO = "full"
_DEBUG = False


def _perm_heads(order):
    pi = np.empty(DH, np.int64)
    pi[0::2] = np.arange(64)
    pi[1::2] = np.arange(64) + 64
    return np.concatenate([h * DH + pi for h in order])


def _perm_blocks(order):
    return np.concatenate([np.arange(h * DH, h * DH + DH) for h in order])


def _host_consts():
    c = {}
    c["idf"] = np.eye(128, dtype=np.float32)
    c["cumtri"] = np.triu(np.ones((128, 128), np.float32)).copy()
    c["onescolb"] = np.ones((128, 1), dtype=BF)
    c["onescolf"] = np.ones((128, 1), dtype=np.float32)
    c["onesrowf"] = np.ones((1, 128), dtype=np.float32)
    e8 = np.zeros((128, 16), np.float32)
    for p in range(128):
        e8[p, p // 8] = 1.0
    c["e8"] = e8
    c["e8t"] = np.ascontiguousarray(e8.T)
    rep = np.zeros((16, 256), np.float32)
    for er in range(256):
        rep[er // 16, er] = 1.0
    c["rep16"] = rep
    dk = np.zeros((128, 128), np.float32)
    for p in range(128):
        dk[p, p + 1:] = -BIGF
    c["dkill"] = dk
    jj = np.arange(512)[None, :]
    pp = np.arange(128)[:, None]
    for dlt in range(-384, 513, 128):
        r = jj + dlt - pp
        c[f"bc_{dlt}"] = ((r >= 0) & (r <= BAND)).astype(BF)
    c["cz_0"] = ((jj - pp) >= 0).astype(BF)
    return c


def _rope_tables(tbase):
    invf = (1.0 / 10000.0 ** (np.arange(0, DH, 2, dtype=np.float32) / DH)).astype(np.float32)
    pos = np.arange(tbase, tbase + HS, dtype=np.float32)
    fr = pos[None, :] * invf[:, None]
    cos = np.cos(fr).astype(np.float32)
    sin = np.sin(fr).astype(np.float32)
    rc = np.empty((128, HS), np.float32)
    rs = np.empty((128, HS), np.float32)
    rc[0::2] = cos
    rc[1::2] = cos
    rs[0::2] = -sin
    rs[1::2] = sin
    return rc, rs


SHUF_SWAP = []
for _g in range(16):
    SHUF_SWAP += [2 * _g + 1, 2 * _g]


_HZ_OPS = frozenset("""tensor_tensor tensor_add tensor_sub tensor_mul tensor_max
tensor_scalar tensor_single_scalar tensor_scalar_add tensor_scalar_sub
tensor_scalar_mul tensor_scalar_max tensor_scalar_min tensor_relu tensor_reduce
reduce_max reduce_sum scalar_tensor_tensor tensor_tensor_scan tensor_copy memset
memzero reciprocal select transpose stream_shuffle bn_stats bn_aggr max max_index
pool pool_avg pool_max match_replace copy_predicated activation mul add copy sign
square sqrt iota partition_broadcast affine_select""".split())


class HZ:

    def __init__(self, eng):
        self._e = eng
        self._w = set()
        self._r = set()

    @staticmethod
    def _nm(a):
        t = getattr(a, "tensor", None)
        return getattr(t, "name", None) if t is not None else None

    def __getattr__(self, n):
        a = getattr(self._e, n)
        if n in _HZ_OPS and callable(a):
            e = self._e
            me = self

            def w(*args, **kwargs):
                outs = set()
                ins = set()
                for i, ag in enumerate(args):
                    m = me._nm(ag)
                    if m:
                        (outs if i == 0 else ins).add(m)
                for k, ag in kwargs.items():
                    m = me._nm(ag)
                    if m:
                        (outs if k == "accum_out" else ins).add(m)
                if (ins & me._w) or (outs & (me._r | me._w)):
                    e.drain()
                    me._w = set()
                    me._r = set()
                r = a(*args, **kwargs)
                me._w |= outs
                me._r |= ins
                return r
            return w
        return a


class Sem:

    def __init__(self, stack, nc, name):
        self.h = stack.enter_context(nc.semaphore(name))
        self.v = 0

    def inc(self, instr, n=1):
        instr.then_inc(self.h, n)
        self.v += n
        return instr

    def wait(self, eng):
        eng.wait_ge(self.h, self.v)

    def rinc(self, q, key, fn, args, n=1, kwargs=None):
        h = self.h
        self.v += n
        q[key].append((fn, args, kwargs or {}, h, n))

    def rwait(self, q, key, eng):
        h, v = self.h, self.v
        q[key].append((eng.wait_ge, (h, v), {}, None, 0))


def _replay(q, key):
    for fn, args, kwargs, semh, n in q[key]:
        r = fn(*args, **kwargs)
        if semh is not None:
            r.then_inc(semh, n)


def build_nc(upto="full", debug=False):
    nc = bass.Bass("TRN2", target_bir_lowering=False)
    dram = lambda n, s, d, k="ExternalInput": nc.dram_tensor(n, s, d, kind=k)

    ins = {}
    for n, s, d in [
        ("xo", [D, HS], F32), ("xh", [D, HS], F32), ("role", [1, 1], I32),
        ("wq", [D, D], F32), ("wk", [D, D], F32), ("wv", [D, D], F32),
        ("wo", [D, D], F32), ("w0", [D, D], F32), ("w1", [D, D], F32),
        ("rw1", [D, 64], F32), ("rw2", [64, 16], F32),
        ("at", [D, 256], F32), ("bt", [256, D], F32),
        ("cT", [64, D], F32), ("bT", [D, 64], F32),
        ("dcol", [128, 16], F32), ("b0col", [128, 16], F32), ("b1col", [128, 16], F32),
        ("lamur", [1, 32], F32), ("lamim", [1, 32], F32), ("logd", [1, 32], F32),
        ("rb1r", [1, 64], F32), ("rb2r", [1, 16], F32), ("gater", [1, 16], F32),
        ("ropec", [128, HS], F32), ("ropes", [128, HS], F32),
        ("idf", [128, 128], F32), ("cumtri", [128, 128], F32),
        ("onescolb", [128, 1], BF16), ("onescolf", [128, 1], F32),
        ("onesrowf", [1, 128], F32),
        ("e8", [128, 16], F32), ("e8t", [16, 128], F32), ("rep16", [16, 256], F32),
        ("dkill", [128, 128], F32), ("cz_0", [128, 512], BF16),
    ]:
        ins[n] = dram(n, s, d)
    for dlt in range(-384, 513, 128):
        ins[f"bc_{dlt}"] = dram(f"bc_{dlt}", [128, 512], BF16)

    out = dram("out", [D, HS], F32, "ExternalOutput")
    kd = "ExternalOutput" if debug else "Internal"
    x1s = nc.dram_tensor("x1s", [D, HS], F32, kind=kd)
    xn1s = nc.dram_tensor("xn1s", [D, HS], BF16, kind=kd)
    qscr = nc.dram_tensor("qscr", [D, HS], BF16, kind=kd)
    bscrd = nc.dram_tensor("bscrd", [D, HS], F32, kind="Internal")
    cc_kv_i = nc.dram_tensor("cc_kv_i", [6272, D], BF16)
    cc_kv_o = nc.dram_tensor("cc_kv_o", [12544, D], BF16)
    cc_t_i = nc.dram_tensor("cc_t_i", [8, S], F32)
    cc_t_o = nc.dram_tensor("cc_t_o", [16, S], F32)
    cc_a_i = nc.dram_tensor("cc_a_i", [1024, S], BF16)
    cc_a_o = nc.dram_tensor("cc_a_o", [2048, S], BF16)
    ckvo_d = nc.dram_tensor("ckvo_d", [12544, D], BF16, kind=kd) if debug else None
    cto_d = nc.dram_tensor("cto_d", [16, S], F32, kind=kd) if debug else None
    attnd = nc.dram_tensor("attnd", [D, HS], BF16, kind=kd)
    xnd = nc.dram_tensor("xnd", [D, HS], BF16, kind=kd)
    f0d = nc.dram_tensor("f0d", [D, HS], F32, kind=kd)
    y0d = nc.dram_tensor("y0d", [D, HS], F32, kind=kd)
    dbgrow = nc.dram_tensor("dbgrow", [1, 1024], F32, kind=kd)
    dbgtile = nc.dram_tensor("dbgtile", [128, 1024], F32, kind=kd)

    est = contextlib.ExitStack()
    with est:
        sbg = lambda n, s, d: est.enter_context(nc.sbuf_tensor(n, s, d))
        pst = lambda n, s: est.enter_context(nc.psum_tensor(n, s, F32))
        mks = lambda n: Sem(est, nc, n)

        C = {}
        for n, s, d in [
            ("idf", [128, 128], F32), ("cumtri", [128, 128], F32),
            ("onescolb", [128, 1], BF16), ("onescolf", [128, 1], F32),
            ("onesrowf", [1, 128], F32),
            ("e8", [128, 16], F32), ("e8t", [16, 128], F32), ("rep16", [16, 256], F32),
            ("dkill", [128, 128], F32), ("dcol", [128, 16], F32),
            ("b0col", [128, 16], F32), ("b1col", [128, 16], F32),
            ("role", [1, 1], I32), ("cz_0", [128, 512], BF16),
        ]:
            C[n] = sbg("c_" + n, s, d)
        for dlt in range(-384, 513, 128):
            C[f"bc_{dlt}"] = sbg(f"c_bc{dlt}", [128, 512], BF16)

        srow = sbg("srow", [1, 1024], F32)
        sml = sbg("sml", [1, 1024], F32)
        p_neg = sbg("p_neg", [128, 64], F32)
        p_pos = sbg("p_pos", [128, 64], F32)
        p_pos1 = sbg("p_pos1", [128, 64], F32)
        a128 = sbg("a128", [1, 64], F32)
        a127 = sbg("a127", [1, 64], F32)
        wbT = sbg("wbT", [128, 16 * 64], BF16)
        cTb = sbg("cTb", [64, D], BF16)
        epsr = sbg("epsr", [1, 1], F32)
        onesall = sbg("onesall", [128, 128], F32)
        WB = sbg("WB", [128, 16 * 2048], BF16)
        WS = [sbg(f"WS{i}", [128, 2048], BF16) for i in range(2)]
        WS2 = [sbg(f"WS2_{i}", [128, 2048], BF16) for i in range(2)]
        EV = [sbg(f"EV{i}", [128, 512], F32) for i in range(3)]
        EVB = sbg("EVB", [128, 2048], BF16)
        scl = sbg("scl", [128, 4], F32)
        rw_a = sbg("rw_a", [1, 512], F32)
        rw_b = sbg("rw_b", [1, 512], F32)
        rw_c = sbg("rw_c", [1, 512], F32)
        negc = sbg("negc", [128, 1], F32)
        negcB = sbg("negcB", [128, 1], F32)
        zacc = sbg("zacc", [128, 1], F32)
        zprt = sbg("zprt", [128, 1], F32)
        invz = sbg("invz", [128, 1], F32)

        pA = [pst(f"pA{i}", [128, 512]) for i in range(2)]
        pB = [pst(f"pB{i}", [128, 512]) for i in range(2)]
        pC = pst("pC", [128, 512])
        pD = pst("pD", [128, 512])
        pE = pst("pE", [128, 512])
        pF = pst("pF", [128, 512])

        sy = mks("sy")
        sg = mks("sg")
        ss = mks("ss")
        sv = mks("sv")
        stn = mks("stn")
        ssc = mks("ssc")
        scc = mks("scc")
        bar = mks("bar")

        V, T, SC, SY, G = HZ(nc.vector), nc.tensor, HZ(nc.scalar), nc.sync, HZ(nc.gpsimd)
        ALL_E = (V, T, SC, SY, G)

        def barrier():
            for e in ALL_E:
                bar.inc(e.nop())
            for e in ALL_E:
                bar.wait(e)

        for n in ("idf", "cumtri", "onescolb", "onescolf", "onesrowf", "e8", "e8t",
                  "rep16", "dkill", "dcol", "b0col", "b1col", "role", "cz_0"):
            sy.inc(SY.dma_start(out=C[n][:], in_=ins[n].ap()), 16)
        for dlt in range(-384, 513, 128):
            sy.inc(SY.dma_start(out=C[f"bc_{dlt}"][:], in_=ins[f"bc_{dlt}"].ap()), 16)
        for dst, name in [((0, 32), "lamur"), ((32, 64), "lamim"), ((64, 96), "logd"),
                          ((96, 160), "rb1r"), ((160, 176), "rb2r"), ((176, 192), "gater")]:
            sy.inc(SY.dma_start(out=srow[0:1, dst[0]:dst[1]], in_=ins[name].ap()), 16)

        rreg = {}
        for e in ALL_E:
            sy.wait(e)
            r = e.alloc_register("rl")
            e.reg_load(r, C["role"][0:1, 0:1])
            rreg[id(e)] = r
        V.memset(epsr[:], EPS_LN)
        V.memset(onesall[:], 1.0)

        with contextlib.ExitStack() as es0:
            sb0 = lambda n, s, d: es0.enter_context(nc.sbuf_tensor(n, s, d))
            pw_re = sb0("pw_re", [32, 160], F32)
            pw_im = sb0("pw_im", [32, 160], F32)
            gcol = sb0("gcol", [32, 8], F32)
            gb = sb0("gb", [128, 128], F32)
            wbf = sb0("wbf", [128, 64], F32)
            bTf = sb0("bTf", [128, 16 * 64], F32)
            cTf = sb0("cTf", [64, D], F32)
            WF = [sb0(f"WF{i}", [128, 2048], F32) for i in range(2)]

            for ks in range(16):
                sy.inc(SY.dma_start(out=bTf[:, 64 * ks:64 * ks + 64],
                                    in_=ins["bT"].ap()[128 * ks:128 * ks + 128, :]), 16)
            sy.inc(SY.dma_start(out=cTf[:], in_=ins["cT"].ap()), 16)

            rr = lambda a, w=32: srow[0:1, a:a + w]
            tt = lambda a, w=32: sml[0:1, a:a + w]
            LUR, LIM, LGD = 0, 32, 64
            DEL, LRE, TH, ARE, AIM, GRE, GIM, L2 = 192, 224, 256, 288, 320, 352, 384, 416

            sy.wait(SC)
            sy.wait(V)
            ssc.inc(SC.activation(rr(DEL), rr(LGD), AF.Exp))
            ssc.inc(SC.activation(tt(0), rr(LUR), AF.Exp))
            ssc.wait(V)
            sv.inc(V.tensor_scalar(tt(0), tt(0), 1.0, None, ALU.add))
            sv.wait(SC)
            ssc.inc(SC.activation(tt(32), tt(0), AF.Ln))
            ssc.wait(V)
            sv.inc(V.tensor_scalar(rr(LRE), tt(32), 0.0101, -1.0, ALU.add, ALU.mult))
            sv.inc(V.tensor_tensor(rr(TH), rr(LIM), rr(DEL), ALU.mult))
            sv.inc(V.tensor_copy(srow[0:1, 672:704], rr(TH)))
            i2p, tp = float(1.0 / (2 * np.pi)), float(2 * np.pi)
            for _it in range(2):
                sv.inc(V.tensor_scalar(tt(0), rr(TH), i2p, 0.5, ALU.mult, ALU.is_ge))
                sv.inc(V.tensor_scalar(tt(32), rr(TH), -i2p, 0.5, ALU.mult, ALU.is_ge))
                if _it == 0:
                    sv.inc(V.tensor_copy(srow[0:1, 704:736], tt(0)))
                    sv.inc(V.tensor_copy(srow[0:1, 736:768], tt(32)))
                sv.inc(V.tensor_tensor(tt(0), tt(0), tt(32), ALU.subtract))
                sv.inc(V.tensor_scalar(tt(0), tt(0), tp, None, ALU.mult))
                sv.inc(V.tensor_tensor(rr(TH), rr(TH), tt(0), ALU.subtract))
            sv.inc(V.tensor_copy(srow[0:1, 448:480], rr(TH)))
            sv.wait(SC)
            ssc.inc(SC.activation(tt(64), rr(TH), AF.Sin, scale=0.5))
            ssc.wait(V)
            sv.inc(V.tensor_copy(srow[0:1, 480:512], tt(64)))
            sv.inc(V.tensor_scalar(tt(0), rr(TH), 0.5, None, ALU.mult))
            sv.inc(V.tensor_scalar(tt(32), rr(TH), -0.5, None, ALU.mult))
            sv.inc(V.tensor_tensor(tt(0), tt(0), tt(32), ALU.max))
            sv.inc(V.tensor_scalar(tt(0), tt(0), -1.0, float(np.pi / 2), ALU.mult, ALU.add))
            sv.inc(V.tensor_copy(srow[0:1, 640:672], tt(0)))
            sv.wait(SC)
            ssc.inc(SC.activation(tt(96), tt(0), AF.Sin))
            ssc.wait(V)
            sv.inc(V.tensor_copy(srow[0:1, 512:544], tt(96)))
            sv.inc(V.tensor_tensor(tt(0), rr(LRE), rr(DEL), ALU.mult))
            sv.wait(SC)
            ssc.inc(SC.activation(tt(128), tt(0), AF.Exp))
            ssc.wait(V)
            sv.inc(V.tensor_tensor(tt(0), tt(64), tt(64), ALU.mult))
            sv.inc(V.tensor_scalar(tt(0), tt(0), -2.0, 1.0, ALU.mult, ALU.add))
            sv.inc(V.tensor_tensor(tt(32), tt(64), tt(96), ALU.mult))
            sv.inc(V.tensor_scalar(tt(32), tt(32), 2.0, None, ALU.mult))
            sv.inc(V.tensor_copy(srow[0:1, 576:608], tt(0)))
            sv.inc(V.tensor_copy(srow[0:1, 608:640], tt(32)))
            sv.inc(V.tensor_tensor(rr(ARE), tt(128), tt(0), ALU.mult))
            sv.inc(V.tensor_tensor(rr(AIM), tt(128), tt(32), ALU.mult))
            sv.inc(V.tensor_tensor(tt(0), rr(LRE), rr(LRE), ALU.mult))
            sv.inc(V.tensor_tensor(tt(32), rr(LIM), rr(LIM), ALU.mult))
            sv.inc(V.tensor_tensor(rr(L2), tt(0), tt(32), ALU.add))
            sv.inc(V.reciprocal(rr(L2), rr(L2)))
            sv.inc(V.tensor_scalar(tt(0), rr(ARE), -1.0, None, ALU.add))
            sv.inc(V.tensor_tensor(tt(32), tt(0), rr(LRE), ALU.mult))
            sv.inc(V.tensor_tensor(tt(64), rr(AIM), rr(LIM), ALU.mult))
            sv.inc(V.tensor_tensor(rr(GRE), tt(32), tt(64), ALU.add))
            sv.inc(V.tensor_tensor(rr(GRE), rr(GRE), rr(L2), ALU.mult))
            sv.inc(V.tensor_tensor(tt(32), rr(AIM), rr(LRE), ALU.mult))
            sv.inc(V.tensor_tensor(tt(64), tt(0), rr(LIM), ALU.mult))
            sv.inc(V.tensor_tensor(rr(GIM), tt(32), tt(64), ALU.subtract))
            sv.inc(V.tensor_tensor(rr(GIM), rr(GIM), rr(L2), ALU.mult))

            def row2col(row, col):
                sv.wait(T)
                stn.inc(T.transpose(pF[0:32, 0:1], row, C["idf"][0:1, 0:1]))
                stn.wait(V)
                sv.inc(V.tensor_copy(col, pF[0:32, 0:1]))

            row2col(rr(ARE), gcol[:, 0:1])
            row2col(rr(AIM), gcol[:, 1:2])

            sv.inc(V.memset(pw_re[:, 0:1], 1.0))
            sv.inc(V.memset(pw_im[:, 0:1], 0.0))
            sv.inc(V.tensor_copy(pw_re[:32, 1:2], gcol[:, 0:1]))
            sv.inc(V.tensor_copy(pw_im[:32, 1:2], gcol[:, 1:2]))
            m = 2
            while m < 137:
                w = min(m, 137 - m)
                sv.inc(V.tensor_scalar(gcol[:, 2:3], pw_re[:32, m - 1:m], gcol[:, 0:1], None, ALU.mult))
                sv.inc(V.tensor_scalar(gcol[:, 3:4], pw_im[:32, m - 1:m], gcol[:, 1:2], None, ALU.mult))
                sv.inc(V.tensor_tensor(gcol[:, 2:3], gcol[:, 2:3], gcol[:, 3:4], ALU.subtract))
                sv.inc(V.tensor_scalar(gcol[:, 4:5], pw_re[:32, m - 1:m], gcol[:, 1:2], None, ALU.mult))
                sv.inc(V.tensor_scalar(gcol[:, 5:6], pw_im[:32, m - 1:m], gcol[:, 0:1], None, ALU.mult))
                sv.inc(V.tensor_tensor(gcol[:, 4:5], gcol[:, 4:5], gcol[:, 5:6], ALU.add))
                sv.inc(V.tensor_scalar(pw_re[:32, m:m + w], pw_re[:32, 0:w], gcol[:, 2:3], None, ALU.mult))
                sv.inc(V.tensor_scalar(wbf[0:32, 0:w], pw_im[:32, 0:w], gcol[:, 4:5], None, ALU.mult))
                sv.inc(V.tensor_tensor(pw_re[:32, m:m + w], pw_re[:32, m:m + w], wbf[0:32, 0:w], ALU.subtract))
                sv.inc(V.tensor_scalar(pw_im[:32, m:m + w], pw_re[:32, 0:w], gcol[:, 4:5], None, ALU.mult))
                sv.inc(V.tensor_scalar(wbf[0:32, 0:w], pw_im[:32, 0:w], gcol[:, 2:3], None, ALU.mult))
                sv.inc(V.tensor_tensor(pw_im[:32, m:m + w], pw_im[:32, m:m + w], wbf[0:32, 0:w], ALU.add))
                m += w
            ngt = sb0("ngt", [32, 384], F32)
            sv.inc(V.tensor_tensor(ngt[:, 256:384], pw_re[:32, 0:128], pw_re[:32, 0:128], ALU.mult))
            sv.inc(V.tensor_tensor(ngt[:, 0:128], pw_im[:32, 0:128], pw_im[:32, 0:128], ALU.mult))
            sv.inc(V.tensor_tensor(ngt[:, 256:384], ngt[:, 256:384], ngt[:, 0:128], ALU.add))
            sv.inc(V.reciprocal(ngt[:, 256:384], ngt[:, 256:384]))
            sv.inc(V.tensor_tensor(ngt[:, 0:128], pw_re[:32, 0:128], ngt[:, 256:384], ALU.mult))
            sv.inc(V.tensor_tensor(ngt[:, 128:256], pw_im[:32, 0:128], ngt[:, 256:384], ALU.mult))
            sv.inc(V.tensor_scalar(ngt[:, 128:256], ngt[:, 128:256], -1.0, None, ALU.mult))

            def tr32(src, dst):
                sv.wait(T)
                stn.inc(T.transpose(pF[0:128, 0:32], src, C["idf"][0:32, 0:32]))
                stn.wait(V)
                sv.inc(V.tensor_copy(dst, pF[0:128, 0:32]))

            tr32(ngt[:, 0:128], p_neg[:, 0:32])
            tr32(ngt[:, 128:256], p_neg[:, 32:64])
            tr32(pw_re[:32, 0:128], p_pos[:, 0:32])
            tr32(pw_im[:32, 0:128], p_pos[:, 32:64])
            tr32(pw_re[:32, 1:129], p_pos1[:, 0:32])
            tr32(pw_im[:32, 1:129], p_pos1[:, 32:64])
            sv.wait(T)
            stn.inc(T.transpose(pF[0:1, 0:32], pw_re[:32, 128:129], C["idf"][0:32, 0:32]))
            stn.inc(T.transpose(pF[0:1, 32:64], pw_im[:32, 128:129], C["idf"][0:32, 0:32]))
            stn.inc(T.transpose(pF[0:1, 64:96], pw_re[:32, 127:128], C["idf"][0:32, 0:32]))
            stn.inc(T.transpose(pF[0:1, 96:128], pw_im[:32, 127:128], C["idf"][0:32, 0:32]))
            stn.wait(V)
            sv.inc(V.tensor_copy(a128[:], pF[0:1, 0:64]))
            sv.inc(V.tensor_copy(a127[:], pF[0:1, 64:128]))

            sv.wait(T)
            stn.inc(T.matmul(pF[0:128, 0:32], C["onesrowf"][:, :], rr(GRE), start=True, stop=True))
            stn.inc(T.matmul(pF[0:128, 32:64], C["onesrowf"][:, :], rr(GIM), start=True, stop=True))
            stn.wait(V)
            sy.wait(V)
            sv.inc(V.tensor_copy(gb[:, 0:64], pF[0:128, 0:64]))
            for ks in range(16):
                b_re = bTf[:, 64 * ks:64 * ks + 32]
                b_im = bTf[:, 64 * ks + 32:64 * ks + 64]
                sv.inc(V.tensor_tensor(wbf[:, 0:32], b_re, gb[:, 0:32], ALU.mult))
                sv.inc(V.tensor_tensor(wbf[:, 32:64], b_im, gb[:, 32:64], ALU.mult))
                sv.inc(V.tensor_tensor(wbf[:, 0:32], wbf[:, 0:32], wbf[:, 32:64], ALU.subtract))
                sv.inc(V.tensor_copy(wbT[:, 64 * ks:64 * ks + 32], wbf[:, 0:32]))
                sv.inc(V.tensor_tensor(wbf[:, 0:32], b_im, gb[:, 0:32], ALU.mult))
                sv.inc(V.tensor_tensor(wbf[:, 32:64], b_re, gb[:, 32:64], ALU.mult))
                sv.inc(V.tensor_tensor(wbf[:, 0:32], wbf[:, 0:32], wbf[:, 32:64], ALU.add))
                sv.inc(V.tensor_copy(wbT[:, 64 * ks + 32:64 * ks + 64], wbf[:, 0:32]))
            sv.inc(V.tensor_copy(cTb[:], cTf[:]))

            sv.inc(V.memset(zacc[:], 0.0))
            for ks in range(16):
                SY.wait_ge(sv.h, sv.v)
                sy.inc(SY.dma_start(out=WF[0][:], in_=ins["w0"].ap()[128 * ks:128 * ks + 128, :]), 16)
                sy.wait(V)
                sv.inc(V.tensor_reduce(zprt[:], WF[0][:], AXL, ALU.add, apply_absolute_value=True))
                sv.inc(V.tensor_tensor(zacc[:], zacc[:], zprt[:], ALU.add))
            sv.wait(T)
            stn.inc(T.matmul(pF[0:1, 0:1], C["onescolf"][:], zacc[:], start=True, stop=True))
            stn.wait(V)
            sv.inc(V.tensor_copy(srow[0:1, 768:769], pF[0:1, 0:1]))
            sv.inc(V.tensor_scalar(sml[0:1, 0:1], pF[0:1, 0:1], float(1.0 / (D * D)), None, ALU.mult))
            sv.inc(V.tensor_scalar(sml[0:1, 1:2], sml[0:1, 0:1], 1.0, 1e-10, ALU.mult, ALU.add))
            sv.inc(V.reciprocal(sml[0:1, 1:2], sml[0:1, 1:2]))
            sv.wait(T)
            stn.inc(T.matmul(pF[0:128, 0:2], C["onesrowf"][:, :], sml[0:1, 0:2], start=True, stop=True))
            stn.wait(V)
            sv.inc(V.tensor_copy(scl[:, 0:2], pF[0:128, 0:2]))
            sv.inc(V.tensor_scalar(scl[:, 2:3], scl[:, 1:2], -1.0, None, ALU.mult))
            for ks in range(16):
                b = ks % 2
                SY.wait_ge(sv.h, sv.v)
                sy.inc(SY.dma_start(out=WF[b][:], in_=ins["w0"].ap()[128 * ks:128 * ks + 128, :]), 16)
                sy.wait(V)
                sv.inc(V.tensor_scalar(WS[0][:], WF[b][:], scl[:, 1:2], 0.5, ALU.mult, ALU.is_ge))
                sv.inc(V.tensor_scalar(WS[1][:], WF[b][:], scl[:, 2:3], 0.5, ALU.mult, ALU.is_ge))
                sv.inc(V.tensor_tensor(WB[:, 2048 * ks:2048 * ks + 2048], WS[0][:], WS[1][:], ALU.subtract))
            SY.wait_ge(sy.h, sy.v)
            barrier()

        with contextlib.ExitStack() as es1:
            sb1 = lambda n, s, d: es1.enter_context(nc.sbuf_tensor(n, s, d))
            AX = sb1("AX", [128, 16 * 512], F32)
            AXB = sb1("AXB", [128, 16 * 512], BF16)
            AXN = sb1("AXN", [128, 16 * 512], BF16)
            buS = sb1("buS", [64, 512], F32)
            buT = sb1("buT", [128, 256], F32)
            xsT = sb1("xsT", [64, 512], BF16)
            carry = sb1("carry", [1, 64], F32)

            V.memset(carry[:], 0.0)

            def emit_ln(xt, xnt):
                sv.wait(T)
                sv.inc(V.tensor_copy(AXB[:], xt[:]))
                sv.wait(T)
                for ks in range(16):
                    stn.inc(T.matmul(pC[0:1, 0:512], C["onescolb"][:],
                                     AXB[:, 512 * ks:512 * ks + 512],
                                     start=(ks == 0), stop=(ks == 15)))
                sv.wait(SC)
                ssc.inc(SC.activation(AXB[:], AXB[:], AF.Square))
                ssc.wait(T)
                for ks in range(16):
                    stn.inc(T.matmul(pC[32:33, 0:512], C["onescolb"][:],
                                     AXB[:, 512 * ks:512 * ks + 512],
                                     start=(ks == 0), stop=(ks == 15), tile_position=(0, 32)))
                stn.wait(V)
                sv.inc(V.tensor_scalar(rw_a[:], pC[0:1, 0:512], float(1.0 / D), None, ALU.mult))
                sv.inc(V.tensor_scalar(rw_b[:], pC[32:33, 0:512], float(1.0 / D), None, ALU.mult))
                sv.inc(V.tensor_tensor(rw_c[:], rw_a[:], rw_a[:], ALU.mult))
                sv.inc(V.tensor_tensor(rw_b[:], rw_b[:], rw_c[:], ALU.subtract))
                sv.wait(SC)
                ssc.inc(SC.activation(rw_b[:], rw_b[:], AF.Ln, bias=epsr[0:1, 0:1]))
                ssc.inc(SC.activation(rw_b[:], rw_b[:], AF.Exp, scale=-0.5))
                ssc.wait(V)
                sv.inc(V.tensor_tensor(rw_a[:], rw_a[:], rw_b[:], ALU.mult))
                sv.wait(T)
                stn.inc(T.matmul(pD[0:128, 0:512], C["onesrowf"][:, :], rw_b[:], start=True, stop=True))
                stn.inc(T.matmul(pE[0:128, 0:512], C["onesrowf"][:, :], rw_a[:], start=True, stop=True))
                stn.wait(V)
                for ks in range(16):
                    sl = slice(512 * ks, 512 * ks + 512)
                    sv.inc(V.tensor_tensor(EV[0][:], xt[:, sl], pD[0:128, 0:512], ALU.mult))
                    sv.inc(V.tensor_tensor(xnt[:, sl], EV[0][:], pE[0:128, 0:512], ALU.subtract))

            def emit_scan(xnt, is_halo):
                sv.wait(T)
                for ks in range(16):
                    stn.inc(T.matmul(pF[0:64, 0:512], wbT[:, 64 * ks:64 * ks + 64],
                                     xnt[:, 512 * ks:512 * ks + 512],
                                     start=(ks == 0), stop=(ks == 15)))
                stn.wait(V)
                sv.inc(V.tensor_copy(buS[:], pF[0:64, 0:512]))
                sv.wait(T)
                for sub in range(4):
                    stn.inc(T.transpose(pF[0:128, 64 * sub:64 * sub + 64],
                                        buS[:, 128 * sub:128 * sub + 128], C["idf"][0:64, 0:64]))
                stn.wait(V)
                sv.inc(V.tensor_copy(buT[:], pF[0:128, 0:256]))
                for sub in range(4):
                    br = buT[:, 64 * sub:64 * sub + 32]
                    bi = buT[:, 64 * sub + 32:64 * sub + 64]
                    cre, cim = EV[0][:, 0:32], EV[0][:, 32:64]
                    t1, t2 = EV[0][:, 64:96], EV[0][:, 96:128]
                    sv.inc(V.tensor_tensor(cre, br, p_neg[:, 0:32], ALU.mult))
                    sv.inc(V.tensor_tensor(t1, bi, p_neg[:, 32:64], ALU.mult))
                    sv.inc(V.tensor_tensor(cre, cre, t1, ALU.subtract))
                    sv.inc(V.tensor_tensor(cim, br, p_neg[:, 32:64], ALU.mult))
                    sv.inc(V.tensor_tensor(t2, bi, p_neg[:, 0:32], ALU.mult))
                    sv.inc(V.tensor_tensor(cim, cim, t2, ALU.add))
                    sv.wait(T)
                    stn.inc(T.matmul(pF[0:128, 256:320], C["cumtri"][:], EV[0][:, 0:64],
                                     start=True, stop=True))
                    stn.inc(T.matmul(pE[0:1, 0:64], C["onescolf"][:], EV[0][:, 0:64],
                                     start=True, stop=True))
                    stn.wait(V)
                    lre, lim = EV[1][:, 0:32], EV[1][:, 32:64]
                    t3, t4 = EV[1][:, 64:96], EV[1][:, 96:128]
                    sv.inc(V.tensor_tensor(lre, pF[0:128, 256:288], p_pos[:, 0:32], ALU.mult))
                    sv.inc(V.tensor_tensor(t3, pF[0:128, 288:320], p_pos[:, 32:64], ALU.mult))
                    sv.inc(V.tensor_tensor(lre, lre, t3, ALU.subtract))
                    sv.inc(V.tensor_tensor(lim, pF[0:128, 256:288], p_pos[:, 32:64], ALU.mult))
                    sv.inc(V.tensor_tensor(t4, pF[0:128, 288:320], p_pos[:, 0:32], ALU.mult))
                    sv.inc(V.tensor_tensor(lim, lim, t4, ALU.add))
                    e5, e6 = sml[0:1, 224:256], sml[0:1, 256:288]
                    sv.inc(V.tensor_tensor(sml[0:1, 0:32], pE[0:1, 0:32], a127[0:1, 0:32], ALU.mult))
                    sv.inc(V.tensor_tensor(e5, pE[0:1, 32:64], a127[0:1, 32:64], ALU.mult))
                    sv.inc(V.tensor_tensor(sml[0:1, 0:32], sml[0:1, 0:32], e5, ALU.subtract))
                    sv.inc(V.tensor_tensor(sml[0:1, 32:64], pE[0:1, 0:32], a127[0:1, 32:64], ALU.mult))
                    sv.inc(V.tensor_tensor(e6, pE[0:1, 32:64], a127[0:1, 0:32], ALU.mult))
                    sv.inc(V.tensor_tensor(sml[0:1, 32:64], sml[0:1, 32:64], e6, ALU.add))
                    if not is_halo:
                        sv.wait(T)
                        stn.inc(T.matmul(pF[0:128, 320:384], C["onesrowf"][:, :], carry[:],
                                         start=True, stop=True))
                        stn.wait(V)
                        cr_, ci_ = pF[0:128, 320:352], pF[0:128, 352:384]
                        sv.inc(V.tensor_tensor(t3, cr_, p_pos1[:, 0:32], ALU.mult))
                        sv.inc(V.tensor_tensor(t4, ci_, p_pos1[:, 32:64], ALU.mult))
                        sv.inc(V.tensor_tensor(t3, t3, t4, ALU.subtract))
                        sv.inc(V.tensor_tensor(lre, lre, t3, ALU.add))
                        sv.inc(V.tensor_tensor(t3, cr_, p_pos1[:, 32:64], ALU.mult))
                        sv.inc(V.tensor_tensor(t4, ci_, p_pos1[:, 0:32], ALU.mult))
                        sv.inc(V.tensor_tensor(t3, t3, t4, ALU.add))
                        sv.inc(V.tensor_tensor(lim, lim, t3, ALU.add))
                        sv.inc(V.tensor_scalar(lre, lre, 2.0, None, ALU.mult))
                        sv.inc(V.tensor_scalar(lim, lim, -2.0, None, ALU.mult))
                        sv.wait(T)
                        stn.inc(T.transpose(pF[0:64, 384:512], EV[1][:, 0:64], C["idf"][:, :]))
                        stn.wait(V)
                        sv.inc(V.tensor_copy(xsT[:, 128 * sub:128 * sub + 128], pF[0:64, 384:512]))
                    c5, c6 = sml[0:1, 64:96], sml[0:1, 96:128]
                    c7, c8 = sml[0:1, 128:160], sml[0:1, 160:192]
                    sv.inc(V.tensor_tensor(c5, carry[0:1, 0:32], a128[0:1, 0:32], ALU.mult))
                    sv.inc(V.tensor_tensor(c6, carry[0:1, 32:64], a128[0:1, 32:64], ALU.mult))
                    sv.inc(V.tensor_tensor(c7, c5, c6, ALU.subtract))
                    sv.inc(V.tensor_tensor(c5, carry[0:1, 0:32], a128[0:1, 32:64], ALU.mult))
                    sv.inc(V.tensor_tensor(c6, carry[0:1, 32:64], a128[0:1, 0:32], ALU.mult))
                    sv.inc(V.tensor_tensor(c8, c5, c6, ALU.add))
                    sv.inc(V.tensor_tensor(carry[0:1, 0:32], sml[0:1, 0:32], c7, ALU.add))
                    sv.inc(V.tensor_tensor(carry[0:1, 32:64], sml[0:1, 32:64], c8, ALU.add))

            for ti in range(8):
                is_halo = ti < 4
                src = ins["xh"] if is_halo else ins["xo"]
                tb = (ti % 4) * 512
                ap_in = bass.AP(src, tb, [[HS, 128], [128 * HS, 16], [1, 512]])
                SY.wait_ge(sv.h, sv.v)
                SY.wait_ge(ss.h, ss.v)
                sy.inc(SY.dma_start(out=AX[:], in_=ap_in), 16)
                sy.wait(V)
                sy.wait(SC)
                sy.wait(T)
                emit_ln(AX, AXN)
                if debug and not is_halo:
                    SC.wait_ge(sv.h, sv.v)
                    ss.inc(SC.dma_start(out=bass.AP(xnd, tb, [[HS, 128], [128 * HS, 16], [1, 512]]),
                                        in_=AXN[:]), 16)
                    V.wait_ge(ss.h, ss.v)
                emit_scan(AXN, is_halo)
                if is_halo:
                    continue
                for od in range(16):
                    pf = pB[od % 2]
                    sv.wait(T)
                    for ks in range(16):
                        stn.inc(T.matmul(pf[:, 0:512],
                                         WB[:, 2048 * ks + 128 * od:2048 * ks + 128 * od + 128],
                                         AXN[:, 512 * ks:512 * ks + 512],
                                         start=(ks == 0), stop=(ks == 15)))
                    stn.inc(T.matmul(pA[0][:, 0:512], cTb[:, 128 * od:128 * od + 128],
                                     xsT[:], start=True, stop=True))
                    stn.wait(V)
                    sl = slice(512 * od, 512 * od + 512)
                    sv.inc(V.tensor_scalar(EV[0][:], pf[:, 0:512], scl[:, 0:1],
                                           C["b0col"][:, od:od + 1], ALU.mult, ALU.add))
                    if debug:
                        sv.inc(V.tensor_copy(EV[2][:], pA[0][:, 0:512]))
                        SC.wait_ge(sv.h, sv.v)
                        ss.inc(SC.dma_start(out=bass.AP(f0d, 128 * od * HS + tb, [[HS, 128], [1, 512]]),
                                            in_=EV[0][:]), 16)
                        ss.inc(SC.dma_start(out=bass.AP(y0d, 128 * od * HS + tb, [[HS, 128], [1, 512]]),
                                            in_=EV[2][:]), 16)
                        V.wait_ge(ss.h, ss.v)
                    sv.inc(V.tensor_tensor(EV[1][:], EV[0][:], pA[0][:, 0:512], ALU.add))
                    sv.inc(V.tensor_tensor(EV[1][:], EV[1][:], AX[:, sl], ALU.add))
                    sv.inc(V.tensor_scalar(EV[0][:], AXN[:, sl], C["dcol"][:, od:od + 1],
                                           None, ALU.mult))
                    sv.inc(V.tensor_tensor(AX[:, sl], EV[0][:], EV[1][:], ALU.add))
                emit_ln(AX, AXN)
                ap_x1 = bass.AP(x1s, tb, [[HS, 128], [128 * HS, 16], [1, 512]])
                ap_xn = bass.AP(xn1s, tb, [[HS, 128], [128 * HS, 16], [1, 512]])
                SC.wait_ge(sv.h, sv.v)
                ss.inc(SC.dma_start(out=ap_x1, in_=AX[:]), 16)
                ss.inc(SC.dma_start(out=ap_xn, in_=AXN[:]), 16)
            if debug:
                SC.wait_ge(sv.h, sv.v)
                ss.inc(SC.dma_start(out=dbgrow.ap(), in_=srow[:]), 16)
                ss.inc(SC.dma_start(out=bass.AP(dbgtile, 0, [[1024, 128], [1, 64]]),
                                    in_=p_pos[:]), 16)
                ss.inc(SC.dma_start(out=bass.AP(dbgtile, 64, [[1024, 128], [1, 64]]),
                                    in_=p_neg[:]), 16)
                ss.inc(SC.dma_start(out=bass.AP(dbgtile, 128, [[1024, 128], [1, 4]]),
                                    in_=scl[:]), 16)
                sg.inc(G.dma_start(out=bass.AP(dbgtile, 132, [[1024, 128], [1, 64]]),
                                   in_=wbT[:, 0:64]), 16)
                G.wait_ge(sg.h, sg.v)
            SC.wait_ge(ss.h, ss.v)
            SY.wait_ge(sy.h, sy.v)
            barrier()

        if upto == "p1":
            _tail(nc, V, SY, out, EV[0], sy)
            return nc

        with contextlib.ExitStack() as es2:
            sb2 = lambda n, s, d: es2.enter_context(nc.sbuf_tensor(n, s, d))
            ropecS = sb2("ropecS", [128, HS], F32)
            ropesS = sb2("ropesS", [128, HS], F32)
            shq = sb2("shq", [128, 512], F32)
            nqmax = sb2("nqmax", [1, 512], F32)
            nkmax = sb2("nkmax", [1, 512], F32)
            statsb = sb2("statsb", [1, 8], BF16)

            ap_xn1f = bass.AP(xn1s, 0, [[HS, 128], [128 * HS, 16], [1, 2048]])
            sy.inc(SY.dma_start(out=WB[:], in_=ap_xn1f), 16)
            sy.inc(SY.dma_start(out=ropecS[:], in_=ins["ropec"].ap()), 16)
            sy.inc(SY.dma_start(out=ropesS[:], in_=ins["ropes"].ap()), 16)
            sy.wait(V)
            sy.wait(T)
            sy.wait(G)
            sv.inc(V.memset(nqmax[:], 0.0))
            sv.inc(V.memset(nkmax[:], 0.0))

            def wblock_ap(w, od):
                return bass.AP(w, 128 * od, [[D, 128], [128 * D, 16], [1, 128]])

            PPs = [pA[0], pA[1], pB[0], pB[1]]

            def emit_qk(w, dst, dst_row, nmax):
                for od in range(16):
                    b = od % 2
                    G.wait_ge(stn.h, stn.v)
                    sg.inc(G.dma_start(out=WS[b][:], in_=wblock_ap(w, od)), 16)
                    sg.wait(T)
                    T.wait_ge(sv.h, sv.v)
                    for tc in range(4):
                        for ks in range(16):
                            stn.inc(T.matmul(PPs[tc][:, 0:512], WS[b][:, 128 * ks:128 * ks + 128],
                                             WB[:, 2048 * ks + 512 * tc:2048 * ks + 512 * tc + 512],
                                             start=(ks == 0), stop=(ks == 15)))
                    stn.wait(V)
                    for tc in range(4):
                        tsl = slice(512 * tc, 512 * tc + 512)
                        sv.inc(V.stream_shuffle(shq[:], PPs[tc][:, 0:512], mask=SHUF_SWAP))
                        sv.inc(V.tensor_tensor(EV[0][:], PPs[tc][:, 0:512], ropecS[:, tsl], ALU.mult))
                        sv.inc(V.tensor_tensor(EV[1][:], shq[:], ropesS[:, tsl], ALU.mult))
                        sv.inc(V.tensor_tensor(EVB[:, 512 * tc:512 * tc + 512], EV[0][:], EV[1][:], ALU.add))
                    sv.wait(SC)
                    ssc.inc(SC.activation(WS2[0][:], EVB[:], AF.Square))
                    ssc.wait(T)
                    for tc in range(4):
                        T.wait_ge(sv.h, sv.v)
                        stn.inc(T.matmul(pC[0:1, 0:512], C["onescolb"][:],
                                         WS2[0][:, 512 * tc:512 * tc + 512], start=True, stop=True))
                        stn.wait(V)
                        sv.inc(V.tensor_tensor(nmax[:], nmax[:], pC[0:1, 0:512], ALU.max))
                    SC.wait_ge(sv.h, sv.v)
                    ss.inc(SC.dma_start(out=bass.AP(dst, dst_row * D + 128 * od * D,
                                                    [[D, 128], [1, 2048]]),
                                        in_=EVB[:]), 16)

            emit_qk(ins["wq"], cc_kv_i, 4096, nqmax)
            emit_qk(ins["wk"], cc_kv_i, 0, nkmax)

            for od in range(16):
                b = od % 2
                G.wait_ge(stn.h, stn.v)
                sg.inc(G.dma_start(out=WS[b][:], in_=wblock_ap(ins["wv"], od)), 16)
                sg.wait(T)
                T.wait_ge(sv.h, sv.v)
                for tc in range(4):
                    for ks in range(16):
                        stn.inc(T.matmul(PPs[tc][:, 0:512], WS[b][:, 128 * ks:128 * ks + 128],
                                         WB[:, 2048 * ks + 512 * tc:2048 * ks + 512 * tc + 512],
                                         start=(ks == 0), stop=(ks == 15)))
                stn.wait(V)
                V.wait_ge(ss.h, ss.v)
                for tc in range(4):
                    sv.inc(V.tensor_copy(EV[2][:], PPs[tc][:, 0:512]))
                    for j in range(4):
                        tb_ = 4 * tc + j
                        pq = pC if tb_ % 2 == 0 else pD
                        sv.wait(T)
                        stn.inc(T.transpose(pq[0:128, 0:128],
                                            EV[2][:, 128 * j:128 * j + 128], C["idf"][:, :]))
                        stn.wait(V)
                        sv.inc(V.tensor_copy(WS2[1][:, 128 * tb_:128 * tb_ + 128], pq[0:128, 0:128]))
                SC.wait_ge(sv.h, sv.v)
                ss.inc(SC.dma_start(
                    out=bass.AP(cc_kv_i, 2048 * D + 128 * od, [[D, 128], [128 * D, 16], [1, 128]]),
                    in_=bass.AP(WS2[1], 0, [[2048, 128], [128, 16], [1, 128]])), 16)


            sv.inc(V.tensor_reduce(sml[0:1, 0:1], nqmax[:], AXL, ALU.max))
            sv.inc(V.tensor_reduce(sml[0:1, 1:2], nkmax[:], AXL, ALU.max))
            sv.inc(V.tensor_copy(statsb[0:1, 0:2], sml[0:1, 0:2]))
            SC.wait_ge(sv.h, sv.v)
            ss.inc(SC.dma_start(out=bass.AP(cc_kv_i, 6144 * D, [[D, 1], [1, 8]]),
                                in_=statsb[:]), 16)
            SC.wait_ge(ss.h, ss.v)
            SY.wait_ge(sy.h, sy.v)
            barrier()

            scc.inc(G.collective_compute(
                "AllGather", ALU.bypass,
                replica_groups=[[0, 1], [2, 3], [4, 5], [6, 7]],
                ins=[cc_kv_i.ap()], outs=[cc_kv_o.ap()]))
            scc.wait(G)
            if debug:
                sg.inc(G.dma_start(out=ckvo_d.ap(), in_=cc_kv_o.ap()), 16)
                G.wait_ge(sg.h, sg.v)
            barrier()

        if upto == "p2":
            _tail(nc, V, SY, out, EV[0], sy)
            return nc

        with contextlib.ExitStack() as es3:
            sb3 = lambda n, sh, d: es3.enter_context(nc.sbuf_tensor(n, sh, d))
            KH = [sb3(f"KH{i}", [128, S], BF16) for i in range(2)]
            VH = [sb3(f"VH{i}", [128, S], BF16) for i in range(2)]
            QH = [sb3(f"QH{i}", [128, S], BF16) for i in range(2)]
            PBu = sb3("PBu", [128, S], BF16)
            P2B = [sb3(f"P2B{i}", [128, 512], BF16) for i in range(2)]
            mscr = sb3("mscr", [128, 512], BF16)
            tmpev = sb3("tmpev", [128, 1024], F32)
            Tall = sb3("Tall", [128, 512], F32)
            bis = sb3("bis", [16, 8], F32)
            mid128 = sb3("mid128", [128, 1], F32)
            cnt128 = sb3("cnt128", [128, 1], F32)
            hvt = sb3("hvt", [128, 512], F32)
            hb01 = [sb3(f"hb01_{j}", [128, 128], F32) for j in range(4)]
            hbias = [sb3(f"hbias{j}", [128, 128], F32) for j in range(4)]
            zrec = sb3("zrec", [128, 512], F32)
            zbc = sb3("zbc", [128, 512], F32)
            R1 = 6272

            sg.inc(G.dma_start(out=sml[0:1, 0:2],
                               in_=bass.AP(cc_kv_o, 6144 * D, [[D, 1], [1, 2]])), 16)
            sg.inc(G.dma_start(out=sml[0:1, 2:4],
                               in_=bass.AP(cc_kv_o, (R1 + 6144) * D, [[D, 1], [1, 2]])), 16)
            sg.wait(V)
            sv.inc(V.tensor_tensor(sml[0:1, 4:5], sml[0:1, 0:1], sml[0:1, 2:3], ALU.max))
            sv.inc(V.tensor_tensor(sml[0:1, 5:6], sml[0:1, 1:2], sml[0:1, 3:4], ALU.max))
            sv.inc(V.tensor_tensor(sml[0:1, 6:7], sml[0:1, 4:5], sml[0:1, 5:6], ALU.add))
            sv.inc(V.tensor_scalar(sml[0:1, 6:7], sml[0:1, 6:7], float(ISD / 2), 2.0,
                                   ALU.mult, ALU.add))
            sv.inc(V.tensor_scalar(sml[0:1, 7:8], sml[0:1, 6:7], -1.0, None, ALU.mult))
            sv.wait(T)
            stn.inc(T.matmul(pF[0:128, 0:1], C["onesrowf"][:, :], sml[0:1, 7:8],
                             start=True, stop=True))
            stn.wait(V)
            sv.inc(V.tensor_copy(negc[:], pF[0:128, 0:1]))
            sv.inc(V.tensor_scalar(negcB[:], negc[:], 1.0, -80.0, ALU.mult, ALU.add))
            barrier()

            p3sems = [sy, sg, ss, sv, stn, ssc]

            def mk_branch(role):
                q = {"v": [], "t": [], "s": [], "y": []}
                last = {}

                def bw(key, eng, bank):
                    if bank in last:
                        h, v = last[bank]
                        q[key].append((eng.wait_ge, (h, v), {}, None, 0))

                def bs(bank, sem):
                    last[bank] = (sem.h, sem.v)

                def rv(key, eng, sem):
                    q[key].append((eng.wait_ge, (sem.h, sem.v), {}, None, 0))

                def loads(s_, bi):
                    lp = 128 * (s_ if role == 0 else s_ + 8)
                    up = 128 * (s_ + 8 if role == 0 else s_)
                    for bank in (f"KH{bi}", f"QH{bi}", f"VH{bi}"):
                        bw("y", SY, bank)
                    sy.rinc(q, "y", SY.dma_start, (), 16,
                            dict(out=KH[bi][:, 0:2048],
                                 in_=bass.AP(cc_kv_o, lp * D, [[D, 128], [1, 2048]])))
                    sy.rinc(q, "y", SY.dma_start, (), 16,
                            dict(out=KH[bi][:, 2048:4096],
                                 in_=bass.AP(cc_kv_o, (R1 + up) * D, [[D, 128], [1, 2048]])))
                    sy.rinc(q, "y", SY.dma_start, (), 16,
                            dict(out=QH[bi][:, 0:2048],
                                 in_=bass.AP(cc_kv_o, (4096 + lp) * D, [[D, 128], [1, 2048]])))
                    sy.rinc(q, "y", SY.dma_start, (), 16,
                            dict(out=QH[bi][:, 2048:4096],
                                 in_=bass.AP(cc_kv_o, (R1 + 4096 + up) * D, [[D, 128], [1, 2048]])))
                    sy.rinc(q, "y", SY.dma_start, (), 16,
                            dict(out=bass.AP(VH[bi], 0, [[S, 128], [128, 16], [1, 128]]),
                                 in_=bass.AP(cc_kv_o, 2048 * D + lp,
                                             [[D, 128], [128 * D, 16], [1, 128]])))
                    sy.rinc(q, "y", SY.dma_start, (), 16,
                            dict(out=bass.AP(VH[bi], 2048, [[S, 128], [128, 16], [1, 128]]),
                                 in_=bass.AP(cc_kv_o, (R1 + 2048) * D + up,
                                             [[D, 128], [128 * D, 16], [1, 128]])))
                    return (sy.h, sy.v)

                return q, last, bw, bs, rv, loads

            def rec_sweep1(role):
                q, last, bw, bs, rv, loads = mk_branch(role)
                ldmark = {}
                ldmark[0] = loads(0, 0)
                for s_ in range(8):
                    bi = s_ % 2
                    if s_ + 1 < 8:
                        ldmark[s_ + 1] = loads(s_ + 1, (s_ + 1) % 2)
                    q["t"].append((T.wait_ge, ldmark[s_], {}, None, 0))
                    rv("v", V, stn)
                    sv.rinc(q, "v", V.memset, (pC[:, 0:512], 0.0))
                    sv.rinc(q, "v", V.memset, (pD[:, 0:512], 0.0))
                    msmark = (sv.h, sv.v)
                    q["t"].append((T.wait_ge, msmark, {}, None, 0))
                    for qt in range(32):
                        kmax = 128 * qt + 128
                        stat = QH[bi][:, 128 * qt:128 * qt + 128]
                        pieces = []
                        k0 = 0
                        while k0 < kmax:
                            kw = min(512, kmax - k0)
                            pieces.append((k0, kw))
                            k0 += kw
                        for pi_, (k0, kw) in enumerate(pieces):
                            pp = pA[pi_ % 2]
                            bw("t", T, f"pA{pi_ % 2}")
                            stn.rinc(q, "t", T.matmul, (pp[:, 0:kw], stat, KH[bi][:, k0:k0 + kw]),
                                     1, dict(start=True, stop=True))
                            last_p = (pi_ == len(pieces) - 1)
                            if last_p:
                                rv("v", V, stn)
                                sv.rinc(q, "v", V.tensor_tensor,
                                        (pp[:, kw - 128:kw], pp[:, kw - 128:kw],
                                         C["dkill"][:], ALU.add))
                                rv("s", SC, sv)
                            rv("s", SC, stn)
                            ssc.rinc(q, "s", SC.activation,
                                     (PBu[:, k0:k0 + kw], pp[:, 0:kw], AF.Exp),
                                     1, dict(bias=negc[:, 0:1], scale=ISD,
                                             accum_out=(zacc[:] if pi_ == 0 else zprt[:])))
                            bs(f"pA{pi_ % 2}", ssc)
                            if pi_ > 0:
                                rv("v", V, ssc)
                                sv.rinc(q, "v", V.tensor_tensor,
                                        (zacc[:], zacc[:], zprt[:], ALU.add))
                                rv("s", SC, sv)
                        rv("v", V, ssc)
                        sv.rinc(q, "v", V.reciprocal, (invz[:], zacc[:]))
                        sv.rinc(q, "v", V.tensor_scalar,
                                (PBu[:, 0:kmax], PBu[:, 0:kmax], invz[:, 0:1], None, ALU.mult))
                        rv("t", T, sv)
                        for j in range((kmax + 511) // 512):
                            w = min(512, kmax - 512 * j)
                            bank = pC if j < 4 else pD
                            part = 32 * (j % 4)
                            stn.rinc(q, "t", T.matmul,
                                     (bank[part:part + 1, 0:w], C["onescolb"][:],
                                      PBu[:, 512 * j:512 * j + w]),
                                     1, dict(start=False, stop=True, tile_position=(0, part)))
                    for bank in (f"KH{bi}", f"QH{bi}", f"VH{bi}"):
                        bs(bank, stn)
                    rv("v", V, stn)
                    bw("v", V, "tmpev")
                    sv.rinc(q, "v", V.tensor_copy, (tmpev[:, 0:512], pC[:, 0:512]))
                    sv.rinc(q, "v", V.tensor_copy, (tmpev[:, 512:1024], pD[:, 0:512]))
                    rv("s", SC, sv)
                    src_ap = bass.AP(tmpev, 0, [[32 * 1024, 4], [512, 2], [1, 512]])
                    dst_ap = bass.AP(cc_t_i, s_ * S, [[512, 4], [2048, 2], [1, 512]])
                    ss.rinc(q, "s", SC.dma_start, (), 16, dict(out=dst_ap, in_=src_ap))
                    bs("tmpev", ss)
                rv("s", SC, ss)
                rv("y", SY, sy)
                return q

            snap = {id(x): x.v for x in p3sems}
            q_up = rec_sweep1(1)
            up_v = {id(x): x.v for x in p3sems}
            for x in p3sems:
                x.v = snap[id(x)]
            q_lo = rec_sweep1(0)
            for x in p3sems:
                assert x.v == up_v[id(x)], "sweep1 branch sem mismatch"
            for key, e in (("t", T), ("s", SC), ("v", V), ("y", SY)):
                with e.If_eq(rreg[id(e)], 1):
                    _replay(q_up, key)
                with e.Else():
                    _replay(q_lo, key)
                e.end_ifs()
            barrier()

            scc.inc(G.collective_compute(
                "AllGather", ALU.bypass,
                replica_groups=[[0, 1], [2, 3], [4, 5], [6, 7]],
                ins=[cc_t_i.ap()], outs=[cc_t_o.ap()]))
            scc.wait(G)
            if debug:
                sg.inc(G.dma_start(out=cto_d.ap(), in_=cc_t_o.ap()), 16)
                G.wait_ge(sg.h, sg.v)
            barrier()
            sy.inc(SY.dma_start(out=Tall[:], in_=bass.AP(cc_t_o, 0, [[512, 128], [1, 512]])), 16)
            sy.wait(V)
            sv.inc(V.memset(bis[:, 0:1], 0.0))
            sv.inc(V.memset(bis[:, 1:2], 4096.0))
            for it in range(NBIS):
                sv.inc(V.tensor_tensor(bis[:, 2:3], bis[:, 0:1], bis[:, 1:2], ALU.add))
                sv.inc(V.tensor_scalar(bis[:, 2:3], bis[:, 2:3], 0.5, None, ALU.mult))
                sv.wait(T)
                stn.inc(T.matmul(pF[0:128, 0:1], C["e8t"][:, :], bis[:, 2:3], start=True, stop=True))
                stn.wait(V)
                sv.inc(V.tensor_copy(mid128[:], pF[0:128, 0:1]))
                sv.inc(V.tensor_scalar(hvt[:], Tall[:], mid128[:, 0:1], 0.0, ALU.is_gt,
                                       ALU.add, accum_out=cnt128[:]))
                sv.wait(T)
                stn.inc(T.matmul(pF[0:16, 4:5], C["e8"][:, :], cnt128[:], start=True, stop=True))
                stn.wait(V)
                sv.inc(V.tensor_scalar(bis[:, 3:4], pF[0:16, 4:5], 408.5, None, ALU.is_ge))
                sv.inc(V.tensor_scalar(bis[:, 4:5], bis[:, 3:4], -1.0, 1.0, ALU.mult, ALU.add))
                sv.inc(V.tensor_tensor(bis[:, 5:6], bis[:, 2:3], bis[:, 0:1], ALU.subtract))
                sv.inc(V.tensor_tensor(bis[:, 5:6], bis[:, 5:6], bis[:, 3:4], ALU.mult))
                sv.inc(V.tensor_tensor(bis[:, 0:1], bis[:, 0:1], bis[:, 5:6], ALU.add))
                sv.inc(V.tensor_tensor(bis[:, 5:6], bis[:, 2:3], bis[:, 1:2], ALU.subtract))
                sv.inc(V.tensor_tensor(bis[:, 5:6], bis[:, 5:6], bis[:, 4:5], ALU.mult))
                sv.inc(V.tensor_tensor(bis[:, 1:2], bis[:, 1:2], bis[:, 5:6], ALU.add))
            sv.wait(T)
            stn.inc(T.matmul(pF[0:128, 0:1], C["e8t"][:, :], bis[:, 0:1], start=True, stop=True))
            stn.wait(V)
            sv.inc(V.tensor_copy(mid128[:], pF[0:128, 0:1]))
            sv.inc(V.tensor_scalar(hvt[:], Tall[:], mid128[:, 0:1], None, ALU.is_gt))
            sv.wait(T)
            for j in range(4):
                stn.inc(T.transpose(pF[0:128, 0:128], hvt[:, 128 * j:128 * j + 128], C["idf"][:, :]))
                stn.wait(V)
                sv.inc(V.tensor_copy(hb01[j][:], pF[0:128, 0:128]))
                sv.inc(V.tensor_scalar(hbias[j][:], hb01[j][:], 80.0, negcB[:, 0:1],
                                       ALU.mult, ALU.add))
                sv.wait(T)
            barrier()

            def rec_sweep2(role):
                q, last, bw, bs, rv, loads = mk_branch(role)
                ldmark = {0: loads(0, 0)}
                for s_ in range(8):
                    bi = s_ % 2
                    g = 8 * role + s_
                    if s_ + 1 < 8:
                        ldmark[s_ + 1] = loads(s_ + 1, (s_ + 1) % 2)
                    q["t"].append((T.wait_ge, ldmark[s_], {}, None, 0))
                    pcnt = 0
                    for qh in range(2):
                        rv("v", V, stn)
                        for pv in (pB[0], pB[1], pE, pF):
                            sv.rinc(q, "v", V.memset, (pv[:, 0:512], 0.0))
                        sv.rinc(q, "v", V.memset, (pC[:, 0:512], 0.0))
                        msmark = (sv.h, sv.v)
                        q["t"].append((T.wait_ge, msmark, {}, None, 0))
                        for kt in range(16 * (qh + 1)):
                            kg = 128 * kt
                            statK = KH[bi][:, kg:kg + 128]
                            statV = VH[bi][:, kg:kg + 128]
                            for pc in range(4):
                                q0 = 2048 * qh + 512 * pc
                                dlt = q0 - kg
                                amin, amax = dlt - 127, dlt + 511
                                if amax < 0:
                                    continue
                                al = max(0, kg - q0)
                                pure_h = amin > BAND
                                pi = pcnt % 2
                                pp = pA[pi]
                                pbuf = P2B[pi]
                                bw("t", T, f"sA{pi}")
                                stn.rinc(q, "t", T.matmul,
                                         (pp[:, 0:512], statK, QH[bi][:, q0:q0 + 512]),
                                         1, dict(start=True, stop=True))
                                col = 8 * g + kt // 4
                                bias = hbias[kt % 4][:, col:col + 1] if pure_h else negc[:, 0:1]
                                rv("s", SC, stn)
                                bw("s", SC, f"P2B{pi}")
                                ssc.rinc(q, "s", SC.activation, (pbuf[:], pp[:, 0:512], AF.Exp),
                                         1, dict(bias=bias, scale=ISD))
                                bs(f"sA{pi}", ssc)
                                need_mask = not pure_h and not (amin >= 0 and amax <= BAND)
                                if need_mask:
                                    rv("v", V, ssc)
                                    hcol = hb01[kt % 4][:, col:col + 1]
                                    if amin >= 0 and amax > BAND:
                                        sv.rinc(q, "v", V.tensor_scalar,
                                                (mscr[:], C[f"bc_{dlt}"][:], hcol, None, ALU.max))
                                        sv.rinc(q, "v", V.tensor_tensor,
                                                (pbuf[:], pbuf[:], mscr[:], ALU.mult))
                                    elif amax <= BAND:
                                        sv.rinc(q, "v", V.tensor_tensor,
                                                (pbuf[:], pbuf[:], C[f"bc_{dlt}"][:], ALU.mult))
                                    else:
                                        sv.rinc(q, "v", V.tensor_scalar,
                                                (mscr[:], C["cz_0"][:], hcol, None, ALU.mult))
                                        sv.rinc(q, "v", V.tensor_tensor,
                                                (mscr[:], mscr[:], C[f"bc_{dlt}"][:], ALU.max))
                                        sv.rinc(q, "v", V.tensor_tensor,
                                                (pbuf[:], pbuf[:], mscr[:], ALU.mult))
                                    rv("t", T, sv)
                                else:
                                    rv("t", T, ssc)
                                pvp = [pB[0], pB[1], pE, pF][pc]
                                stn.rinc(q, "t", T.matmul,
                                         (pvp[:, al:512], statV, pbuf[:, al:512]),
                                         1, dict(start=False, stop=True))
                                stn.rinc(q, "t", T.matmul,
                                         (pC[32 * pc:32 * pc + 1, al:512], C["onescolb"][:],
                                          pbuf[:, al:512]),
                                         1, dict(start=False, stop=True,
                                                 tile_position=(0, 32 * pc)))
                                bs(f"P2B{pi}", stn)
                                pcnt += 1
                        rv("v", V, stn)
                        sv.rinc(q, "v", V.reciprocal, (zrec[:], pC[:, 0:512]))
                        for pc in range(4):
                            pvp = [pB[0], pB[1], pE, pF][pc]
                            rv("t", T, sv)
                            stn.rinc(q, "t", T.matmul,
                                     (pD[0:128, 0:512], onesall[32 * pc:32 * pc + 1, 0:128],
                                      zrec[32 * pc:32 * pc + 1, 0:512]),
                                     1, dict(start=True, stop=True,
                                             tile_position=(32 * pc, 0)))
                            rv("v", V, stn)
                            sv.rinc(q, "v", V.tensor_copy, (zbc[:], pD[0:128, 0:512]))
                            dst = WB[:, 4096 * s_ + 2048 * qh + 512 * pc:
                                     4096 * s_ + 2048 * qh + 512 * pc + 512]
                            sv.rinc(q, "v", V.tensor_tensor, (dst, pvp[:, 0:512], zbc[:], ALU.mult))
                    for bank in (f"KH{bi}", f"QH{bi}", f"VH{bi}"):
                        bs(bank, stn)
                    rv("s", SC, sv)
                    ss.rinc(q, "s", SC.dma_start, (), 16,
                            dict(out=bass.AP(cc_a_i, 128 * s_ * S, [[S, 128], [1, S]]),
                                 in_=WB[:, 4096 * s_:4096 * s_ + 4096]))
                rv("s", SC, ss)
                rv("y", SY, sy)
                return q

            snap = {id(x): x.v for x in p3sems}
            q_up = rec_sweep2(1)
            up_v = {id(x): x.v for x in p3sems}
            for x in p3sems:
                x.v = snap[id(x)]
            q_lo = rec_sweep2(0)
            for x in p3sems:
                assert x.v == up_v[id(x)], "sweep2 branch sem mismatch"
            for key, e in (("t", T), ("s", SC), ("v", V), ("y", SY)):
                with e.If_eq(rreg[id(e)], 1):
                    _replay(q_up, key)
                with e.Else():
                    _replay(q_lo, key)
                e.end_ifs()
            barrier()
            scc.inc(G.collective_compute(
                "AllGather", ALU.bypass,
                replica_groups=[[0, 1], [2, 3], [4, 5], [6, 7]],
                ins=[cc_a_i.ap()], outs=[cc_a_o.ap()]))
            scc.wait(G)
            barrier()
            with SY.If_eq(rreg[id(SY)], 1):
                for hh in range(16):
                    sy.inc(SY.dma_start(out=WB[:, 2048 * hh:2048 * hh + 2048],
                                        in_=bass.AP(cc_a_o, 128 * hh * S + 2048,
                                                    [[S, 128], [1, 2048]])), 16)
            with SY.Else():
                for hh in range(16):
                    sy.inc(SY.dma_start(out=WB[:, 2048 * hh:2048 * hh + 2048],
                                        in_=bass.AP(cc_a_o, 128 * hh * S,
                                                    [[S, 128], [1, 2048]])), 16)
                for hh in range(16):
                    SY.sem_inc(sy.h, 0) if False else None
            SY.end_ifs()
            sy.v -= 16 * 16
            sy.wait(SY)
            if debug:
                SC.wait_ge(sy.h, sy.v)
                for hh in range(16):
                    ss.inc(SC.dma_start(out=bass.AP(attnd, 128 * hh * HS, [[HS, 128], [1, HS]]),
                                        in_=WB[:, 2048 * hh:2048 * hh + 2048]), 16)
                SC.wait_ge(ss.h, ss.v)
            barrier()

        if upto == "p3":
            _tail(nc, V, SY, out, EV[0], sy)
            return nc

        with contextlib.ExitStack() as es4:
            sb4 = lambda n, s, d: es4.enter_context(nc.sbuf_tensor(n, s, d))
            AXN4 = sb4("AXN4", [128, 16 * 512], BF16)
            rw1b = sb4("rw1b", [128, 16 * 64], BF16)
            rw2b = sb4("rw2b", [64, 16], BF16)
            atb = sb4("atb", [128, 16 * 256], BF16)
            btb = [sb4(f"btb{i}", [128, 2048], BF16) for i in range(2)]
            rb2gb = sb4("rb2gb", [128, 16], F32)
            hTf = sb4("hTf", [64, 512], F32)
            hTb = sb4("hTb", [64, 512], BF16)
            coefT = sb4("coefT", [16, 512], F32)
            cexp = sb4("cexp", [128, 512], F32)
            wxa = sb4("wxa", [128, 2 * 2048], BF16)
            lgt = sb4("lgt", [128, 16], F32)
            lgt2 = sb4("lgt2", [128, 16], F32)
            lsel = sb4("lsel", [128, 16], F32)
            lcol = [sb4(f"lcol{i}", [128, 1], F32) for i in range(3)]
            X1B = sb4("X1B", [128, 2048], F32)
            BSB = sb4("BSB", [128, 2048], F32)

            sg.inc(G.dma_start(out=rw1b[:], in_=bass.AP(ins["rw1"], 0,
                                                        [[64, 128], [128 * 64, 16], [1, 64]])), 16)
            sg.inc(G.dma_start(out=rw2b[:], in_=ins["rw2"].ap()), 16)
            sg.inc(G.dma_start(out=atb[:], in_=bass.AP(ins["at"], 0,
                                                       [[256, 128], [128 * 256, 16], [1, 256]])), 16)
            for i in range(2):
                sg.inc(G.dma_start(out=btb[i][:], in_=ins["bt"].ap()[128 * i:128 * i + 128, :]), 16)
            sg.wait(V)
            sv.inc(V.tensor_tensor(sml[0:1, 0:16], srow[0:1, 160:176], srow[0:1, 176:192], ALU.add))
            sv.wait(T)
            stn.inc(T.matmul(pF[0:128, 0:16], C["onesrowf"][:, :], sml[0:1, 0:16],
                             start=True, stop=True))
            stn.wait(V)
            sv.inc(V.tensor_copy(rb2gb[:], pF[0:128, 0:16]))

            for tc in range(4):
                ap_xn = bass.AP(xn1s, 512 * tc, [[HS, 128], [128 * HS, 16], [1, 512]])
                SY.wait_ge(sv.h, sv.v)
                sy.inc(SY.dma_start(out=AXN4[:], in_=ap_xn), 16)
                sy.wait(T)
                sy.wait(V)
                sv.wait(T)
                ssc.wait(T)
                for ks in range(16):
                    stn.inc(T.matmul(pC[0:64, 0:512], rw1b[:, 64 * ks:64 * ks + 64],
                                     AXN4[:, 512 * ks:512 * ks + 512],
                                     start=(ks == 0), stop=(ks == 15)))
                stn.wait(SC)
                ssc.inc(SC.activation(hTf[:], pC[0:64, 0:512], AF.Exp, scale=-1.0))
                ssc.wait(V)
                sv.inc(V.tensor_scalar(hTf[:], hTf[:], 1.0, None, ALU.add))
                sv.inc(V.reciprocal(hTf[:], hTf[:]))
                sv.inc(V.tensor_tensor(hTb[:], pC[0:64, 0:512], hTf[:], ALU.mult))
                for tsub in range(4):
                    sv.wait(T)
                    stn.inc(T.matmul(pD[0:128, 0:16], hTb[:, 128 * tsub:128 * tsub + 128],
                                     rw2b[:], start=True, stop=True))
                    stn.wait(V)
                    sv.inc(V.tensor_tensor(lgt[:], pD[0:128, 0:16], rb2gb[:], ALU.add))
                    sv.inc(V.reduce_max(lcol[0][:], lgt[:], AXL))
                    sv.inc(V.tensor_scalar(lgt2[:], lgt[:], lcol[0][:, 0:1], None, ALU.subtract))
                    sv.inc(V.tensor_scalar(lsel[:], lgt2[:], 0.0, None, ALU.is_ge))
                    sv.inc(V.tensor_scalar(lgt[:], lsel[:], -1e30, None, ALU.mult))
                    sv.inc(V.tensor_tensor(lgt[:], lgt[:], lgt2[:], ALU.add))
                    sv.inc(V.reduce_max(lcol[1][:], lgt[:], AXL))
                    sv.inc(V.tensor_scalar(lsel[:], lgt2[:], lcol[1][:, 0:1], None, ALU.is_ge))
                    sv.wait(SC)
                    ssc.inc(SC.activation(lgt[:], lgt2[:], AF.Exp))
                    ssc.inc(SC.activation(lcol[2][:], lcol[1][:], AF.Exp))
                    ssc.wait(V)
                    sv.inc(V.tensor_scalar(lcol[2][:], lcol[2][:], 1.0, None, ALU.add))
                    sv.inc(V.reciprocal(lcol[2][:], lcol[2][:]))
                    sv.inc(V.tensor_tensor(lgt[:], lgt[:], lsel[:], ALU.mult))
                    sv.inc(V.tensor_scalar(lgt2[:], lgt[:], lcol[2][:, 0:1], None, ALU.mult))
                    sv.wait(T)
                    stn.inc(T.transpose(pE[0:16, 0:128], lgt2[:], C["idf"][:, :]))
                    stn.wait(V)
                    sv.inc(V.tensor_copy(coefT[:, 128 * tsub:128 * tsub + 128], pE[0:16, 0:128]))
                for i in range(2):
                    sv.wait(T)
                    stn.inc(T.matmul(pD[0:128, 0:512], C["rep16"][:, 128 * i:128 * i + 128],
                                     coefT[:], start=True, stop=True))
                    for ks in range(16):
                        stn.inc(T.matmul(pB[i][:, 0:512],
                                         atb[:, 256 * ks + 128 * i:256 * ks + 128 * i + 128],
                                         AXN4[:, 512 * ks:512 * ks + 512],
                                         start=(ks == 0), stop=(ks == 15)))
                    stn.wait(V)
                    sv.inc(V.tensor_copy(cexp[:], pD[0:128, 0:512]))
                    sv.inc(V.tensor_tensor(wxa[:, 2048 * i + 512 * tc:2048 * i + 512 * tc + 512],
                                           pB[i][:, 0:512], cexp[:], ALU.mult))
                for od in range(16):
                    b = od % 2
                    G.wait_ge(stn.h, stn.v)
                    sg.inc(G.dma_start(out=WS[b][:], in_=wblock_ap2(ins["w1"], od)), 16)
                    sg.wait(T)
                    T.wait_ge(sv.h, sv.v)
                    pp = pA[od % 2]
                    for ks in range(16):
                        stn.inc(T.matmul(pp[:, 0:512], WS[b][:, 128 * ks:128 * ks + 128],
                                         AXN4[:, 512 * ks:512 * ks + 512],
                                         start=(ks == 0), stop=False))
                    for i in range(2):
                        stn.inc(T.matmul(pp[:, 0:512], btb[i][:, 128 * od:128 * od + 128],
                                         wxa[:, 2048 * i + 512 * tc:2048 * i + 512 * tc + 512],
                                         start=False, stop=(i == 1)))
                    stn.wait(V)
                    V.wait_ge(ss.h, ss.v)
                    sv.inc(V.tensor_scalar(EV[2][:], pp[:, 0:512], 1.0,
                                           C["b1col"][:, od:od + 1], ALU.mult, ALU.add))
                    SC.wait_ge(sv.h, sv.v)
                    ss.inc(SC.dma_start(out=bass.AP(bscrd, 128 * od * HS + 512 * tc,
                                                    [[HS, 128], [1, 512]]),
                                        in_=EV[2][:]), 16)
            SC.wait_ge(ss.h, ss.v)
            barrier()

            for od in range(16):
                b = od % 2
                G.wait_ge(stn.h, stn.v)
                sg.inc(G.dma_start(out=WS2[b][:], in_=wblock_ap2(ins["wo"], od)), 16)
                SY.wait_ge(sv.h, sv.v)
                sy.inc(SY.dma_start(out=X1B[:], in_=x1s.ap()[128 * od:128 * od + 128, :]), 16)
                sy.inc(SY.dma_start(out=BSB[:], in_=bscrd.ap()[128 * od:128 * od + 128, :]), 16)
                sg.wait(T)
                T.wait_ge(sv.h, sv.v)
                for tc in range(4):
                    pp = pA[tc % 2]
                    T.wait_ge(sv.h, sv.v)
                    for hh in range(16):
                        stn.inc(T.matmul(pp[:, 0:512], WS2[b][:, 128 * hh:128 * hh + 128],
                                         WB[:, 2048 * hh + 512 * tc:2048 * hh + 512 * tc + 512],
                                         start=(hh == 0), stop=(hh == 15)))
                    stn.wait(V)
                    sy.wait(V)
                    V.wait_ge(ss.h, ss.v)
                    sv.inc(V.tensor_tensor(EV[2][:], pp[:, 0:512],
                                           X1B[:, 512 * tc:512 * tc + 512], ALU.add))
                    sv.inc(V.tensor_tensor(EV[2][:], EV[2][:],
                                           BSB[:, 512 * tc:512 * tc + 512], ALU.add))
                    SC.wait_ge(sv.h, sv.v)
                    ss.inc(SC.dma_start(out=bass.AP(out, 128 * od * HS + 512 * tc,
                                                    [[HS, 128], [1, 512]]),
                                        in_=EV[2][:]), 16)
            SC.wait_ge(ss.h, ss.v)
            barrier()
    return nc


def wblock_ap2(w, od):
    return bass.AP(w, 128 * od, [[D, 128], [128 * D, 16], [1, 128]])


def _tail(nc, V, SY, out, buf, sy):
    V.memset(buf[:], 0.0)
    V.engine_nop().then_inc(sy.h, 1)
    sy.v += 1
    SY.wait_ge(sy.h, sy.v)
    for od in range(16):
        for tc in range(4):
            SY.dma_start(out=bass.AP(out, 128 * od * HS + 512 * tc,
                                     [[HS, 128], [1, 512]]), in_=buf[:]).then_inc(sy.h, 16)
            sy.v += 16
    SY.wait_ge(sy.h, sy.v)


def _prep_inputs(inputs):
    cst = _host_consts()
    f32 = lambda a: np.ascontiguousarray(np.asarray(a, np.float32))
    wqT = np.asarray(inputs["L1_attn_Wq"], np.float32).T
    wkT = np.asarray(inputs["L1_attn_Wk"], np.float32).T
    wvT = np.asarray(inputs["L1_attn_Wv"], np.float32).T
    orders = [list(range(16)), list(range(8, 16)) + list(range(8))]
    wq_c = [f32(wqT[:, _perm_heads(o)]) for o in orders]
    wk_c = [f32(wkT[:, _perm_heads(o)]) for o in orders]
    wv_c = [f32(wvT[:, _perm_blocks(o)]) for o in orders]
    wo = f32(np.asarray(inputs["L1_attn_Wo"], np.float32).T)
    w0 = f32(np.asarray(inputs["L0_ffn_weight"], np.float32).T)
    w1 = f32(np.asarray(inputs["L1_ffn_W"], np.float32).T)
    rw1 = f32(np.asarray(inputs["L1_ffn_rw1"], np.float32).T)
    rw2 = f32(np.asarray(inputs["L1_ffn_rw2"], np.float32).T[:, :16])
    at = f32(np.asarray(inputs["L1_ffn_A"], np.float32).reshape(256, D).T)
    bt = f32(np.asarray(inputs["L1_ffn_B"], np.float32).transpose(1, 0, 2).reshape(D, 256).T)
    cT = f32(np.concatenate([np.asarray(inputs["L0_attn_C_re"], np.float32).T,
                             np.asarray(inputs["L0_attn_C_im"], np.float32).T], 0))
    bT = f32(np.concatenate([np.asarray(inputs["L0_attn_B_re"], np.float32).T,
                             np.asarray(inputs["L0_attn_B_im"], np.float32).T], 1))
    dcol = f32(np.asarray(inputs["L0_attn_D"], np.float32).reshape(16, 128).T)
    b0col = f32(np.asarray(inputs["L0_ffn_bias"], np.float32).reshape(16, 128).T)
    b1col = f32(np.asarray(inputs["L1_ffn_bias"], np.float32).reshape(16, 128).T)
    shared = dict(
        wo=wo, w0=w0, w1=w1, rw1=rw1, rw2=rw2, at=at, bt=bt,
        cT=cT, bT=bT, dcol=dcol, b0col=b0col, b1col=b1col,
        lamur=f32(inputs["L0_attn_Lambda_unconstrained_re"]).reshape(1, 32),
        lamim=f32(inputs["L0_attn_Lambda_im"]).reshape(1, 32),
        logd=f32(np.asarray(inputs["L0_attn_log_Delta"], np.float32)[:32]).reshape(1, 32),
        rb1r=f32(inputs["L1_ffn_rb1"]).reshape(1, 64),
        rb2r=f32(np.asarray(inputs["L1_ffn_rb2"], np.float32)[:16]).reshape(1, 16),
        gater=f32(inputs["L1_ffn_gates"]).reshape(1, 16),
        **cst,
    )
    x = np.asarray(inputs["L0_attn_u"], np.float32)
    in_maps = []
    ropes = [_rope_tables(0), _rope_tables(HS)]
    for c in range(8):
        b_, hf = c // 2, c % 2
        xT = np.ascontiguousarray(x[b_].T)
        m = dict(shared)
        m["xo"] = np.ascontiguousarray(xT[:, HS * hf:HS * (hf + 1)])
        m["xh"] = (np.zeros((D, HS), np.float32) if hf == 0
                   else np.ascontiguousarray(xT[:, 0:HS]))
        m["role"] = np.array([[hf]], np.int32)
        m["ropec"], m["ropes"] = ropes[hf]
        m["wq"], m["wk"], m["wv"] = wq_c[hf], wk_c[hf], wv_c[hf]
        in_maps.append(m)
    return in_maps


def kernel(**inputs):
    in_maps = _prep_inputs(inputs)
    nc = build_nc(_UPTO, _DEBUG)
    res = run_bass_kernel_spmd(nc, in_maps, core_ids=list(range(8)))
    out = np.empty((4, S, D), np.float32)
    for c in range(8):
        b_, hf = c // 2, c % 2
        out[b_, HS * hf:HS * (hf + 1), :] = res.results[c]["out"].T
    return out
